# version 09
# baseline (speedup 1.0000x reference)
import numpy as np
import ml_dtypes
from contextlib import ExitStack
import concourse.bass as bass
import concourse.mybir as mybir
from concourse.bass_utils import run_bass_kernel_spmd

F32 = mybir.dt.float32
BF16 = mybir.dt.bfloat16
AF = mybir.ActivationFunctionType
ALU = mybir.AluOpType
AX = mybir.AxisListType
f32 = np.float32
MAGIC = f32(1.5 * 2 ** 23)

NSEQ, S, D, H, DH, HID = 4, 2048, 1024, 16, 64, 4096
NT = S // 128
NQB = S // 512
NKC = S // 128
NFC = D // 128
NHC = HID // 128
NPAIR = H // 2


class Sem:
    __slots__ = ("h", "n", "name")

    def __init__(self, h, name):
        self.h, self.n, self.name = h, 0, name


class Prog:
    ENGS = ("sync", "gpsimd", "vector", "scalar", "tensor")

    def __init__(self, nc):
        self.nc = nc
        self.streams = {k: [] for k in self.ENGS}
        self.rel = {}

    def emit(self, eng, fn):
        self.streams[eng].append(fn)

    def op(self, eng, fn, inc=None, n=1):
        if inc is not None:
            inc.n += n
            v = inc.n

            def run(e, fn=fn, inc=inc, n=n):
                fn(e).then_inc(inc.h, n)
            self.emit(eng, run)
            return v
        self.emit(eng, lambda e, fn=fn: fn(e))
        return None

    def wait(self, eng, sem, val):
        if val is None or val <= 0:
            return
        self.emit(eng, lambda e, s=sem, v=val: e.wait_ge(s.h, v))

    def release(self, key, sem, val):
        self.rel[key] = [(sem, val)]

    def release_multi(self, key, pairs):
        self.rel[key] = list(pairs)

    def wait_rel(self, eng, key):
        for sem, val in self.rel.get(key, ()):
            self.wait(eng, sem, val)


def build_nc(prefix="full", nseq=NSEQ, a6_mode="full"):
    nc = bass.Bass("TRN2", target_bir_lowering=False)

    x_d = nc.dram_tensor("L0_attn_x", [NSEQ, S, D], F32, kind="ExternalInput")
    wq_d = nc.dram_tensor("L0_attn_Wq", [D, D], F32, kind="ExternalInput")
    wk_d = nc.dram_tensor("L0_attn_Wk", [D, D], F32, kind="ExternalInput")
    wv_d = nc.dram_tensor("L0_attn_Wv", [D, D], F32, kind="ExternalInput")
    wo_d = nc.dram_tensor("L0_attn_Wo", [D, D], F32, kind="ExternalInput")
    fb_d = nc.dram_tensor("L0_ffn_fast_b", [16, 16], F32, kind="ExternalInput")
    wfc_d = nc.dram_tensor("L0_ffn_w_fc", [HID, D], F32, kind="ExternalInput")
    wpj_d = nc.dram_tensor("L0_ffn_w_proj", [D, HID], F32, kind="ExternalInput")
    u_d = nc.dram_tensor("L0_ffn_u", [HID, 16], F32, kind="ExternalInput")
    v16_d = nc.dram_tensor("L0_ffn_v", [16, D], F32, kind="ExternalInput")
    gate_d = nc.dram_tensor("L0_ffn_gate", [1, 1], F32, kind="ExternalInput")
    y_d = nc.dram_tensor("y", [NSEQ, S, D], F32, kind="ExternalOutput")

    wq_bf = nc.dram_tensor("wq_bf", [D, D], BF16, kind="Internal")
    wk_bf = nc.dram_tensor("wk_bf", [D, D], BF16, kind="Internal")
    wv_bf = nc.dram_tensor("wv_bf", [D, D], BF16, kind="Internal")
    wo_bf = nc.dram_tensor("wo_bf", [D, D], BF16, kind="Internal")
    wtot_bf = nc.dram_tensor("wtot_bf", [HID, D], BF16, kind="Internal")
    wp_bf = nc.dram_tensor("wp_bf", [D, HID], BF16, kind="Internal")
    xn_bf = nc.dram_tensor("xn_bf", [NSEQ, S, D], BF16, kind="Internal")
    xn2_bf = nc.dram_tensor("xn2_bf", [NSEQ, S, D], BF16, kind="Internal")
    x1_f = nc.dram_tensor("x1_f", [NSEQ, S, D], F32, kind="Internal")

    ones_row_d = nc.inline_tensor(np.ones((1, 128), np.float32), name="ones_row_c")
    ident_d = nc.inline_tensor(np.eye(128, dtype=np.float32), name="ident_c")

    es = ExitStack()
    sb = lambda n, s, d: es.enter_context(nc.sbuf_tensor(n, s, d))

    AR0 = sb("AR0", [128, 16384], BF16)
    AR1 = sb("AR1", [128, 16384], BF16)
    AR2 = sb("AR2", [128, 16384], BF16)
    AR3 = sb("AR3", [128, 16384], BF16)
    AR4 = sb("AR4", [128, 16384], BF16)
    U1 = sb("U1", [128, 2048], F32)
    U2 = sb("U2", [128, 2048], F32)
    XST = sb("XST", [128, 2048], BF16)
    DENR = sb("DENR", [128, 512], mybir.dt.float32r)
    CONR = sb("CONR", [128, 128], mybir.dt.float32r)
    RCP = sb("RCP", [128, 1024], F32)
    STT = sb("STT", [128, 144], F32)
    DUMP = sb("DUMP", [128, 512], F32)
    CON = sb("CON", [128, 192], F32)
    ONES64 = sb("ONES64", [128, 64], BF16)
    IDT = sb("IDT", [128, 128], F32)
    PS = es.enter_context(nc.psum_tensor("PS", [128, 4096], F32))

    sems = {}
    for name in ("sd", "gd", "tn", "ve", "se", "gm", "av"):
        sems[name] = Sem(es.enter_context(nc.semaphore(name)), name)
    P = Prog(nc)
    sd, gd, tn, ve, se, gm, av = (sems[k] for k in ("sd", "gd", "tn", "ve", "se", "gm", "av"))

    AR4F = AR4[:, 12288:16384].bitcast(F32)

    def dma(eng, out, in_, sem, boundary=True):
        v = P.op(eng, lambda e, o=out, i=in_: e.dma_start(out=o, in_=i), inc=sem, n=16)
        if boundary:
            P.wait(eng, sem, v)
        return v

    def dmat(out, in_, boundary=False):
        v = P.op("sync", lambda e, o=out, i=in_: e.dma_start_transpose(out=o, in_=i), inc=sd, n=16)
        if boundary:
            P.wait("sync", sd, v)
        return v

    def mm(out, lhsT, rhs, start, stop, tile=None, inc=None):
        return P.op("tensor",
                    lambda e, o=out, l=lhsT, r=rhs, st=start, sp=stop, tp=tile:
                    e.matmul(out=o, lhsT=l, rhs=r, start=st, stop=sp, tile_position=tp),
                    inc=inc)

    def ln_stats(b, inv_n):
        c = lambda k, w=1: STT[:, b + k: b + k + w]
        P.op("vector", lambda e, c=c: e.tensor_scalar(out=c(2, 2), in0=c(0, 2), scalar1=float(f32(inv_n)),
                                                      scalar2=None, op0=ALU.mult), inc=ve)
        P.wait("vector", ve, ve.n)
        P.op("vector", lambda e, c=c: e.scalar_tensor_tensor(out=c(4), in0=c(2), scalar=c(2),
                                                             in1=c(3), op0=ALU.mult, op1=ALU.subtract), inc=ve)
        P.wait("scalar", ve, ve.n)
        P.op("scalar", lambda e, c=c: e.activation(out=c(5), in_=c(4), func=AF.Sqrt,
                                                   scale=f32(-1.0), bias=f32(1e-5)), inc=se)
        P.wait("vector", se, se.n)
        P.op("vector", lambda e, c=c: e.reciprocal(out=c(6), in_=c(5)), inc=ve)
        P.wait("vector", ve, ve.n)
        P.op("vector", lambda e, c=c: e.scalar_tensor_tensor(out=c(7), in0=c(2), scalar=-1.0,
                                                             in1=c(6), op0=ALU.mult, op1=ALU.mult), inc=ve)
        return ve.n

    def ve_rsqrt(b):
        c = lambda k, w=1: STT[:, b + k: b + k + w]
        P.op("vector", lambda e, c=c: e.tensor_scalar(
            out=c(5), in0=c(4), scalar1=-1.0, scalar2=1e-5, op0=ALU.mult, op1=ALU.add), inc=ve)
        P.wait("vector", ve, ve.n)
        vi = STT[:, b + 5: b + 6].bitcast(mybir.dt.int32)
        yi = STT[:, b + 6: b + 7].bitcast(mybir.dt.int32)
        P.op("vector", lambda e: e.tensor_scalar(out=yi, in0=vi, scalar1=1, scalar2=None,
                                                 op0=ALU.arith_shift_right), inc=ve)
        P.wait("vector", ve, ve.n)
        P.op("vector", lambda e: e.tensor_scalar(out=yi, in0=yi, scalar1=0, scalar2=None,
                                                 op0=ALU.bitwise_not), inc=ve)
        P.wait("vector", ve, ve.n)
        P.op("vector", lambda e: e.tensor_scalar(out=yi, in0=yi, scalar1=0x5F3759E0, scalar2=None,
                                                 op0=ALU.add), inc=ve)
        P.wait("vector", ve, ve.n)
        for it, (src, dst) in enumerate(((6, 8), (8, 6))):
            P.op("vector", lambda e, c=c, s_=src: e.tensor_mul(out=c(9), in0=c(s_), in1=c(s_)), inc=ve)
            P.wait("vector", ve, ve.n)
            P.op("vector", lambda e, c=c: e.tensor_mul(out=c(9), in0=c(9), in1=c(5)), inc=ve)
            P.wait("vector", ve, ve.n)
            P.op("vector", lambda e, c=c: e.tensor_scalar(
                out=c(10), in0=c(9), scalar1=-0.5, scalar2=1.5, op0=ALU.mult, op1=ALU.add), inc=ve)
            P.wait("vector", ve, ve.n)
            P.op("vector", lambda e, c=c, s_=src, d_=dst: e.tensor_mul(
                out=c(d_), in0=c(s_), in1=c(10)), inc=ve)
            P.wait("vector", ve, ve.n)
        P.op("vector", lambda e, c=c: e.scalar_tensor_tensor(out=c(7), in0=c(2), scalar=-1.0,
                                                             in1=c(6), op0=ALU.mult, op1=ALU.mult), inc=ve)
        return ve.n

    dma("sync", CON[0:1, 0:128], ones_row_d[:, :], sd)
    dma("sync", CON[64:65, 0:128], ones_row_d[:, :], sd)
    dma("sync", IDT[:, :], ident_d[:, :], sd)
    dma("sync", CON[0:1, 130:131], gate_d[:, :], sd)
    con_ld = sd.n
    P.op("gpsimd", lambda e: e.memset(ONES64[:], 1.0), inc=gm)
    ones64_ok = gm.n

    P.wait("tensor", sd, con_ld)
    gbc = mm(PS[:, 0:1], CON[0:1, 0:128], CON[0:1, 130:131], True, True, tile=(0, 0), inc=tn)
    P.wait("vector", tn, gbc)
    gate_ok = P.op("vector", lambda e: e.tensor_copy(out=CON[:, 128:129], in_=PS[:, 0:1]), inc=ve)
    P.release(("qkb", 0), ve, gate_ok)
    P.wait("vector", sd, con_ld)
    P.op("vector", lambda e: e.tensor_copy(out=CONR[0:1, :], in_=CON[0:1, 0:128]))
    conr_ok = P.op("vector", lambda e: e.tensor_copy(out=CONR[64:65, :], in_=CON[64:65, 0:128]), inc=ve)

    wcast_done = {}
    for wdram, wbf, nm in ((wq_d, wq_bf, "q"), (wk_d, wk_bf, "k"), (wv_d, wv_bf, "v"), (wo_d, wo_bf, "o")):
        for ch in range(NFC):
            slot = ch % 2
            P.wait_rel("gpsimd", ("xstw", slot))
            g = dma("gpsimd", XST[:, slot * 1024:(slot + 1) * 1024], wdram[128 * ch:128 * (ch + 1), :], gd)
            P.wait("sync", gd, g)
            sv = dma("sync", wbf[128 * ch:128 * (ch + 1), :], XST[:, slot * 1024:(slot + 1) * 1024], sd)
            P.release(("xstw", slot), sd, sv)
        wcast_done[nm] = sd.n
    xst0_rel = P.rel.pop(("xstw", 0), None)
    xst1_rel = P.rel.pop(("xstw", 1), None)
    if xst0_rel:
        P.release_multi(("xst", 0), xst0_rel)
    if xst1_rel:
        P.release_multi(("xst", 1), xst1_rel)

    last_attnN_val = 0

    STAGES = ["setup", "a1", "a23", "a4", "a6", "b0", "b1ld", "full"]
    lim = STAGES.index(prefix)

    def _finish():
        P.wait("gpsimd", gd, gd.n)
        with es, nc.Block() as blk:
            blk.sync(lambda e: [f(e) for f in P.streams["sync"]])
            blk.gpsimd(lambda e: [f(e) for f in P.streams["gpsimd"]])
            blk.vector(lambda e: [f(e) for f in P.streams["vector"]])
            blk.scalar(lambda e: [f(e) for f in P.streams["scalar"]])
            blk.tensor(lambda e: [f(e) for f in P.streams["tensor"]])
        return nc

    if lim < 1:
        return _finish()

    for s in range(nseq):
        xn_store_last = 0
        for tt in range(NT):
            slot = tt % 2
            u1ap = U1[:, slot * 1024:(slot + 1) * 1024]
            P.wait_rel("sync", ("u1", slot))
            lv = dma("sync", u1ap, x_d[s, 128 * tt:128 * (tt + 1), :], sd)
            b = (tt % 2) * 8
            P.wait_rel("vector", ("a1stt", slot))
            P.wait("vector", sd, lv)
            P.op("vector", lambda e, a=u1ap, b_=b: e.reduce_sum(out=STT[:, b_:b_ + 1], in_=a, axis=AX.X), inc=ve)
            red_v = ve.n
            P.wait("scalar", sd, lv)
            P.wait_rel("scalar", ("u2", slot))
            P.op("scalar", lambda e, a=u1ap, b_=b, sl=slot: e.activation(
                out=U2[:, sl * 1024:(sl + 1) * 1024], in_=a, func=AF.Square,
                accum_out=STT[:, b_ + 1:b_ + 2]), inc=se)
            sq_v = se.n
            P.wait("vector", ve, red_v)
            P.wait("vector", se, sq_v)
            rdy = ln_stats(b, 1.0 / D)
            P.wait_rel("scalar", ("xst", slot))
            P.wait("scalar", ve, rdy)
            ap_v = P.op("scalar", lambda e, a=u1ap, b_=b, sl=slot: e.activation(
                out=XST[:, sl * 1024:(sl + 1) * 1024], in_=a, func=AF.Identity,
                scale=STT[:, b_ + 6:b_ + 7], bias=STT[:, b_ + 7:b_ + 8]), inc=se)
            P.release_multi(("u1", slot), [(se, ap_v), (ve, red_v)])
            P.release(("a1stt", slot), se, ap_v)
            P.release(("u2", slot), se, sq_v)
            P.wait("gpsimd", se, ap_v)
            xn_store_last = dma("gpsimd", xn_bf[s, 128 * tt:128 * (tt + 1), :],
                                XST[:, slot * 1024:(slot + 1) * 1024], gd)
            P.release(("xst", slot), gd, xn_store_last)

        if lim < 2:
            continue
        P.wait("sync", tn, tn.n)
        P.wait("sync", se, se.n)
        P.wait("sync", ve, ve.n)
        P.wait("sync", av, av.n)
        P.wait("sync", gd, xn_store_last)
        xn_ld = 0
        for jf in range(NFC):
            xn_ld = dmat(AR4[:, jf * 2048:(jf + 1) * 2048], xn_bf[s, 0:S, 128 * jf:128 * (jf + 1)],
                         boundary=(jf == NFC - 1))
        P.wait("sync", sd, wcast_done["q"])
        wq_ld = 0
        for jf in range(NFC):
            wq_ld = dmat(AR3[:, jf * 1024:(jf + 1) * 1024], wq_bf[0:D, 128 * jf:128 * (jf + 1)],
                         boundary=(jf == NFC - 1))
        P.wait("sync", sd, wcast_done["k"])
        wk_ld = 0
        for jf in range(NFC):
            wk_ld = dmat(AR3[:, 8192 + jf * 1024:8192 + (jf + 1) * 1024], wk_bf[0:D, 128 * jf:128 * (jf + 1)],
                         boundary=(jf == NFC - 1))

        def qk_proj(dst_ar, w_off, w_ld_val):
            P.wait("tensor", sd, max(w_ld_val, xn_ld))
            P.wait_rel("tensor", ("slot", 0))
            P.wait_rel("tensor", ("slot", 1))
            last_ev = 0
            for oi in range(NFC):
                for qp in range(4):
                    bank = (oi * 4 + qp) % 4
                    P.wait_rel("tensor", ("qkb", bank))
                    last = 0
                    for fi in range(NFC):
                        last = mm(PS[:, bank * 512:(bank + 1) * 512],
                                  AR3[:, w_off + fi * 1024 + oi * 128: w_off + fi * 1024 + (oi + 1) * 128],
                                  AR4[:, fi * 2048 + qp * 512: fi * 2048 + (qp + 1) * 512],
                                  fi == 0, fi == NFC - 1, inc=(tn if fi == NFC - 1 else None))
                    P.wait("scalar", tn, last)
                    last_ev = P.op("scalar", lambda e, b=bank, o=oi, q=qp, d=dst_ar: e.copy(
                        out=d[:, o * 2048 + q * 512: o * 2048 + (q + 1) * 512],
                        in_=PS[:, b * 512:(b + 1) * 512]), inc=se)
                    P.release(("qkb", bank), se, last_ev)
            return last_ev

        q_ev = qk_proj(AR0, 0, wq_ld)
        q_mm_done = tn.n
        k_ev = qk_proj(AR1, 8192, wk_ld)
        k_mm_done = tn.n

        P.wait("sync", tn, q_mm_done)
        P.wait("sync", se, q_ev)
        wv_ld = 0
        for jf in range(NFC):
            wv_ld = dmat(AR3[:, jf * 1024:(jf + 1) * 1024], wv_bf[0:D, 128 * jf:128 * (jf + 1)],
                         boundary=(jf == NFC - 1))

        P.wait("tensor", sd, wv_ld)
        P.wait_rel("tensor", ("avb", 0))
        P.wait_rel("tensor", ("avb", 1))
        v_ev = 0
        v_last_mm = 0
        for tt2 in range(NT):
            pair = tt2 % 2
            pb = 4 + pair * 2
            P.wait_rel("tensor", ("vb", pair))
            last = 0
            for fi in range(NFC):
                for n2 in range(2):
                    last = mm(PS[:, pb * 512 + n2 * 512: pb * 512 + (n2 + 1) * 512],
                              AR4[:, fi * 2048 + tt2 * 128: fi * 2048 + (tt2 + 1) * 128],
                              AR3[:, fi * 1024 + n2 * 512: fi * 1024 + (n2 + 1) * 512],
                              fi == 0, fi == NFC - 1,
                              inc=(tn if (fi == NFC - 1 and n2 == 1) else None))
            v_last_mm = last
            P.wait("vector", tn, last)
            v_ev = P.op("vector", lambda e, t=tt2, p=pb: e.tensor_copy(
                out=AR2[:, t * 1024:(t + 1) * 1024], in_=PS[:, p * 512:(p + 2) * 512]), inc=ve)
            P.release(("vb", pair), ve, v_ev)

        if lim < 3:
            continue
        P.wait("sync", tn, v_last_mm)
        P.wait("sync", sd, wcast_done["o"])
        wo_ld = 0
        for jf in range(NFC):
            wo_ld = dmat(AR4[:, 4096 + jf * 1024: 4096 + (jf + 1) * 1024],
                         wo_bf[0:D, 128 * jf:128 * (jf + 1)], boundary=(jf == NFC - 1))
        P.wait("tensor", se, k_ev)
        P.wait("tensor", ve, v_ev)
        P.wait("tensor", gm, ones64_ok)
        P.wait("tensor", ve, conr_ok)
        P.wait("tensor", sd, wo_ld)
        q_slot_seq = [0]
        pending = None
        pair_par = [0]
        AVBASE = {0: (2048, 2560), 1: (3072, 3584)}
        last_sig_val = 0
        last_av_val = 0
        wo_guard = [0]
        attnN_vals = {}

        def alloc_slot():
            sq = q_slot_seq[0]
            q_slot_seq[0] += 1
            sl = sq % 2
            P.wait_rel("tensor", ("slot", sl))
            return sl

        def flush_pending():
            nonlocal pending, last_attnN_val
            if pending is None:
                return
            pj, pqb, den_val, par = pending
            AVA, AVB = AVBASE[par]
            sl = alloc_slot()
            P.wait("tensor", ve, den_val)
            mm(PS[:, sl * 1024: sl * 1024 + 512], CONR[64:65, 0:128], DENR[64:65, 0:512],
               True, True, tile=(64, 0))
            bc_val = mm(PS[:, sl * 1024 + 512: sl * 1024 + 1024], CONR[0:1, 0:128], DENR[0:1, 0:512],
                        True, True, tile=(0, 0), inc=tn)
            P.wait("vector", tn, bc_val)
            rcp_val = P.op("vector", lambda e, o=sl: e.reciprocal(
                out=RCP[:, 0:1024], in_=PS[:, o * 1024:(o + 1) * 1024]), inc=ve)
            P.release(("slot", sl), ve, rcp_val)
            P.wait("vector", ve, rcp_val)
            P.wait("vector", tn, wo_guard[0])
            P.op("vector", lambda e, pj_=pj, A=AVA: e.tensor_mul(
                out=AR4[0:64, pj_ * 512: pj_ * 512 + 512],
                in0=PS[0:64, A:A + 512], in1=RCP[0:64, 0:512]), inc=ve)
            attnN_val = P.op("vector", lambda e, pj_=pj, B=AVB: e.tensor_mul(
                out=AR4[64:128, pj_ * 512: pj_ * 512 + 512],
                in0=PS[64:128, B:B + 512], in1=RCP[64:128, 512:1024]), inc=ve)
            P.release(("avb", par), ve, attnN_val)
            attnN_vals[pj] = attnN_val
            last_attnN_val = attnN_val
            pending = None

        pending_av = None
        qb_av_guard = av.n

        def emit_av_chunk(pj, pqb, psig, c, par):
            nonlocal last_av_val
            AVA, AVB = AVBASE[par]
            av_last_loc = 0
            for kc in range(4 * c, 4 * c + 4):
                P.wait("tensor", se, psig[kc])
                if kc == 0:
                    P.wait_rel("tensor", ("avb", par))
                wA = AR3[:, kc * 1024: kc * 1024 + 512]
                wB = AR3[:, kc * 1024 + 512: (kc + 1) * 1024]
                mm(PS[0:64, AVA:AVA + 512],
                   AR2[:, kc * 1024 + (2 * pj) * 64: kc * 1024 + (2 * pj) * 64 + 64],
                   wA, kc == 0, kc == NKC - 1, tile=(0, 0), inc=av)
                mm(PS[64:128, AVA:AVA + 512], ONES64[:, :], wA, kc == 0, kc == NKC - 1, tile=(0, 64), inc=av)
                mm(PS[0:64, AVB:AVB + 512], ONES64[:, :], wB, kc == 0, kc == NKC - 1, tile=(0, 0), inc=av)
                av_last_loc = mm(PS[64:128, AVB:AVB + 512],
                                 AR2[:, kc * 1024 + (2 * pj + 1) * 64: kc * 1024 + (2 * pj + 1) * 64 + 64],
                                 wB, kc == 0, kc == NKC - 1, tile=(0, 64), inc=av)
            last_av_val = av_last_loc
            return av_last_loc

        def emit_den_and_pend(pj, pqb, av_last_loc, par):
            nonlocal pending
            AVA, AVB = AVBASE[par]
            P.wait("vector", av, av_last_loc)
            P.op("vector", lambda e, A=AVA: e.tensor_scalar_add(
                out=DENR[64:65, 0:512], in0=PS[64:65, A:A + 512], scalar1=1e-6))
            den_val = P.op("vector", lambda e, B=AVB: e.tensor_scalar_add(
                out=DENR[0:1, 0:512], in0=PS[0:1, B:B + 512], scalar1=1e-6), inc=ve)
            pending = (pj, pqb, den_val, par)

        def emit_score_kc(j, qb, kc, sig_val, av_guard):
            nonlocal last_sig_val
            sl = alloc_slot()
            mm(PS[:, sl * 1024: sl * 1024 + 512],
               AR1[0:64, j * 2048 + kc * 128: j * 2048 + (kc + 1) * 128],
               AR0[0:64, j * 2048 + qb * 512: j * 2048 + (qb + 1) * 512],
               True, True, tile=(0, 0))
            sc_val = mm(PS[:, sl * 1024 + 512: sl * 1024 + 1024],
                        AR1[64:128, j * 2048 + kc * 128: j * 2048 + (kc + 1) * 128],
                        AR0[64:128, j * 2048 + qb * 512: j * 2048 + (qb + 1) * 512],
                        True, True, tile=(64, 0), inc=tn)
            P.wait("scalar", tn, sc_val)
            if av_guard > 0:
                P.wait("scalar", av, av_guard)
            sv = P.op("scalar", lambda e, sl_=sl, kc_=kc: e.activation(
                out=AR3[:, kc_ * 1024:(kc_ + 1) * 1024],
                in_=PS[:, sl_ * 1024:(sl_ + 1) * 1024],
                func=AF.Sigmoid, scale=f32(0.125)), inc=se)
            sig_val[kc] = sv
            last_sig_val = sv
            P.release(("slot", sl), se, sv)

        SC_BEFORE = {0: 1, 1: 3, 2: 5, 3: 7}
        WO_AFTER = {1: 0, 3: 1, 5: 2, 7: 3}
        pending_wo = None
        epi_state = {}
        for qb in range(NQB):
            for j in range(NPAIR):
                flush_pending()
                if pending_wo is not None and 1 <= j <= 6:
                    st1f, st2f, st3f = pending_wo[1]
                    t3_ = j - 3
                    if 0 <= t3_ <= 3:
                        st3f(t3_, epi_state.pop(t3_ + 100))
                    t2_ = j - 2
                    if 0 <= t2_ <= 3:
                        epi_state[t2_ + 100] = st2f(t2_, epi_state.pop(t2_))
                    t1_ = j - 1
                    if 0 <= t1_ <= 3:
                        epi_state[t1_] = st1f(t1_)
                    if j == 6:
                        pending_wo = None
                sig_val = {}
                nsc = 0
                av_base = av.n if pending_av is not None else -1
                for c in range(4):
                    while nsc < SC_BEFORE[c]:
                        g = (av_base + 4 * (nsc + 1)) if av_base >= 0 else qb_av_guard
                        emit_score_kc(j, qb, nsc, sig_val, g)
                        nsc += 1
                        if pending_wo is not None and j == 0 and nsc in WO_AFTER:
                            pending_wo[0](WO_AFTER[nsc])
                    if pending_av is not None:
                        pj, psig, ppar = pending_av
                        avl = emit_av_chunk(pj, qb, psig, c, ppar)
                        if c == 3:
                            emit_den_and_pend(pj, qb, avl, ppar)
                            flush_pending()
                while nsc < NKC:
                    g = (av_base + 4 * (nsc + 1)) if av_base >= 0 else qb_av_guard
                    emit_score_kc(j, qb, nsc, sig_val, g)
                    nsc += 1
                    if pending_wo is not None and j == 0 and nsc in WO_AFTER:
                        pending_wo[0](WO_AFTER[nsc])
                pending_av = (j, dict(sig_val), pair_par[0])
                pair_par[0] ^= 1
            pj, psig, ppar = pending_av
            for c in range(4):
                avl = emit_av_chunk(pj, qb, psig, c, ppar)
            emit_den_and_pend(pj, qb, avl, ppar)
            flush_pending()
            pending_av = None
            qb_av_guard = av.n
            anv = dict(attnN_vals)

            def make_wo(qb=qb, anv=anv):
                def ao_ap(tl):
                    if tl < 2:
                        return U2[:, tl * 1024:(tl + 1) * 1024]
                    return AR4F[:, (tl - 2) * 1024:(tl - 1) * 1024]

                def ao_key(tl):
                    return ("u2", tl) if tl < 2 else ("aof", tl - 2)

                def emit_wo(tl):
                    if tl == 0:
                        for jf in range(NFC):
                            P.wait("tensor", ve, anv.get(jf, 0))
                        P.wait_rel("tensor", ("avb", 0))
                        P.wait_rel("tensor", ("avb", 1))
                    last = 0
                    lasts = [0, 0]
                    for h in range(2):
                        bk = 2048 + ((2 * tl + h) % 4) * 512
                        P.wait_rel("tensor", ("wob", (2 * tl + h) % 4))
                        for jf in range(NFC):
                            lasts[h] = mm(PS[:, bk: bk + 512],
                                          AR4[:, jf * 512 + tl * 128: jf * 512 + (tl + 1) * 128],
                                          AR4[:, 4096 + jf * 1024 + h * 512: 4096 + jf * 1024 + (h + 1) * 512],
                                          jf == 0, jf == NFC - 1,
                                          inc=(tn if jf == NFC - 1 else None))
                    b = 16 + tl * 12
                    ao = ao_ap(tl)
                    P.wait_rel("vector", ao_key(tl))
                    P.wait_rel("vector", ("a4stt1", tl))
                    cps = []
                    for h in range(2):
                        bk = 2048 + ((2 * tl + h) % 4) * 512
                        P.wait("vector", tn, lasts[h])
                        cps.append(P.op("vector", lambda e, h_=h, a=ao, b_=b, bk_=bk: e.tensor_scalar(
                            out=a[:, h_ * 512:(h_ + 1) * 512],
                            in0=PS[:, bk_: bk_ + 512],
                            scalar1=1.0, scalar2=0.0, op0=ALU.mult, op1=ALU.add,
                            accum_out=STT[:, b_ + 8 + 2 * h_: b_ + 9 + 2 * h_]), inc=ve))
                        P.release(("wob", (2 * tl + h) % 4), ve, cps[h])
                    if tl == 3:
                        P.release(("avb", 0), ve, cps[1])
                        P.release(("avb", 1), ve, cps[1])
                    return ao
                return emit_wo

            def make_epi(s=s, qb=qb):
                def ao_ap(tl):
                    if tl < 2:
                        return U2[:, tl * 1024:(tl + 1) * 1024]
                    return AR4F[:, (tl - 2) * 1024:(tl - 1) * 1024]

                def ao_key(tl):
                    return ("u2", tl) if tl < 2 else ("aof", tl - 2)

                def st1(tl):
                    tt = qb * 4 + tl
                    uslot = tl % 2
                    ao = ao_ap(tl)
                    b = 16 + tl * 12
                    c = lambda k, w=1: STT[:, b + k: b + k + w]
                    P.wait_rel("sync", ("u1", uslot))
                    xlv = dma("sync", U1[:, uslot * 1024:(uslot + 1) * 1024],
                              x_d[s, 128 * tt:128 * (tt + 1), :], sd)
                    for h in range(2):
                        P.op("vector", lambda e, h_=h, a=ao: e.tensor_mul(
                            out=DUMP[:, :], in0=a[:, h_ * 512:(h_ + 1) * 512],
                            in1=a[:, h_ * 512:(h_ + 1) * 512]), inc=ve)
                        P.wait("vector", ve, ve.n)
                        P.op("vector", lambda e, h_=h, b_=b: e.reduce_sum(
                            out=STT[:, b_ + 9 + 2 * h_: b_ + 10 + 2 * h_], in_=DUMP[:, :], axis=AX.X), inc=ve)
                        P.wait("vector", ve, ve.n)
                    P.op("vector", lambda e, b_=b: e.tensor_add(
                        out=STT[:, b_: b_ + 2], in0=STT[:, b_ + 8: b_ + 10],
                        in1=STT[:, b_ + 10: b_ + 12]), inc=ve)
                    P.wait("vector", ve, ve.n)
                    P.op("vector", lambda e, c=c: e.tensor_scalar(out=c(2, 2), in0=c(0, 2),
                                                                  scalar1=float(f32(1.0 / D)),
                                                                  scalar2=None, op0=ALU.mult), inc=ve)
                    P.wait("vector", ve, ve.n)
                    vn = P.op("vector", lambda e, c=c: e.scalar_tensor_tensor(
                        out=c(4), in0=c(2), scalar=c(2), in1=c(3),
                        op0=ALU.mult, op1=ALU.subtract), inc=ve)
                    return (xlv, vn)

                def st2(tl, state):
                    xlv, vn = state
                    tt = qb * 4 + tl
                    uslot = tl % 2
                    ao = ao_ap(tl)
                    u1ap = U1[:, uslot * 1024:(uslot + 1) * 1024]
                    b = 16 + tl * 12
                    b2 = 64 + tl * 12
                    P.wait("vector", ve, vn)
                    ve_rsqrt(b)
                    P.wait("vector", ve, ve.n)
                    P.wait("vector", sd, xlv)
                    P.op("vector", lambda e, a=u1ap, b_=b: e.tensor_scalar_add(
                        out=a, in0=a, scalar1=STT[:, b_ + 7:b_ + 8]), inc=ve)
                    P.wait("vector", ve, ve.n)
                    x1_v = P.op("vector", lambda e, a=ao, xp=u1ap, b_=b: e.scalar_tensor_tensor(
                        out=a, in0=a, scalar=STT[:, b_ + 6:b_ + 7], in1=xp,
                        op0=ALU.mult, op1=ALU.add), inc=ve)
                    P.release(("a4stt1", tl), ve, x1_v)
                    P.release(("u1", uslot), ve, x1_v)
                    P.wait("gpsimd", ve, x1_v)
                    x1st = dma("gpsimd", x1_f[s, 128 * tt:128 * (tt + 1), :], ao, gd)
                    P.wait_rel("vector", ("a4stt2", tl))
                    P.wait("vector", ve, x1_v)
                    P.op("vector", lambda e, a=ao, b_=b2: e.reduce_sum(
                        out=STT[:, b_:b_ + 1], in_=a, axis=AX.X), inc=ve)
                    P.wait("vector", ve, ve.n)
                    for h in range(2):
                        P.op("vector", lambda e, h_=h, a=ao: e.tensor_mul(
                            out=DUMP[:, :], in0=a[:, h_ * 512:(h_ + 1) * 512],
                            in1=a[:, h_ * 512:(h_ + 1) * 512]), inc=ve)
                        P.wait("vector", ve, ve.n)
                        P.op("vector", lambda e, h_=h, b_=b2: e.reduce_sum(
                            out=STT[:, b_ + 9 + 2 * h_: b_ + 10 + 2 * h_], in_=DUMP[:, :], axis=AX.X), inc=ve)
                        P.wait("vector", ve, ve.n)
                    P.op("vector", lambda e, b_=b2: e.tensor_add(
                        out=STT[:, b_ + 1: b_ + 2], in0=STT[:, b_ + 9: b_ + 10],
                        in1=STT[:, b_ + 11: b_ + 12]), inc=ve)
                    P.wait("vector", ve, ve.n)
                    c2 = lambda k, w=1: STT[:, b2 + k: b2 + k + w]
                    P.op("vector", lambda e, c=c2: e.tensor_scalar(out=c(2, 2), in0=c(0, 2),
                                                                   scalar1=float(f32(1.0 / D)),
                                                                   scalar2=None, op0=ALU.mult), inc=ve)
                    P.wait("vector", ve, ve.n)
                    vn2 = P.op("vector", lambda e, c=c2: e.scalar_tensor_tensor(
                        out=c(4), in0=c(2), scalar=c(2), in1=c(3),
                        op0=ALU.mult, op1=ALU.subtract), inc=ve)
                    return (x1st, vn2)

                def st3(tl, state):
                    x1st, vn2 = state
                    tt = qb * 4 + tl
                    uslot = tl % 2
                    ao = ao_ap(tl)
                    b2 = 64 + tl * 12
                    P.wait("vector", ve, vn2)
                    ve_rsqrt(b2)
                    P.wait("vector", ve, ve.n)
                    P.wait_rel("vector", ("xst", uslot))
                    ap2_v = P.op("vector", lambda e, a=ao, b_=b2, sl=uslot: e.tensor_scalar(
                        out=XST[:, sl * 1024:(sl + 1) * 1024], in0=a,
                        scalar1=STT[:, b_ + 2:b_ + 3] if False else STT[:, b2 + 2:b2 + 3],
                        scalar2=STT[:, b2 + 6:b2 + 7],
                        op0=ALU.subtract, op1=ALU.mult), inc=ve)
                    P.release(("a4stt2", tl), ve, ap2_v)
                    P.release_multi(ao_key(tl), [(gd, x1st), (ve, ap2_v)])
                    P.wait("gpsimd", ve, ap2_v)
                    g = dma("gpsimd", xn2_bf[s, 128 * tt:128 * (tt + 1), :],
                            XST[:, uslot * 1024:(uslot + 1) * 1024], gd)
                    P.release(("xst", uslot), gd, g)
                return st1, st2, st3

            pending_wo = (make_wo(), make_epi())
            epi_state = {}
            attnN_vals = {}
        if pending_wo is not None:
            for tl in range(4):
                pending_wo[0](tl)
            st1f, st2f, st3f = pending_wo[1]
            for step in range(6):
                t3_ = step - 2
                if 0 <= t3_ <= 3:
                    st3f(t3_, epi_state.pop(t3_ + 100))
                t2_ = step - 1
                if 0 <= t2_ <= 3:
                    epi_state[t2_ + 100] = st2f(t2_, epi_state.pop(t2_))
                if 0 <= step <= 3:
                    epi_state[step] = st1f(step)
            pending_wo = None
            epi_state = {}

    if lim < 5:
        return _finish()
    a_tn, a_se, a_ve, a_gd, a_sd = tn.n, se.n, ve.n, gd.n, sd.n
    a_av = av.n
    for eng in ("gpsimd", "sync", "tensor"):
        P.wait(eng, tn, a_tn)
        P.wait(eng, se, a_se)
        P.wait(eng, ve, a_ve)
        P.wait(eng, av, a_av)
    fb_ld = dma("gpsimd", AR0[0:16, 9216:9232], fb_d[:, :], gd)
    vv_ld = dma("gpsimd", AR0[0:16, 8192:9216], v16_d[:, :], gd)
    ut_ev = 0
    for c in range(NHC):
        slot = c % 2
        P.wait_rel("sync", ("ub", slot))
        if c < 2:
            P.wait_rel("sync", ("u2", 0))
            P.wait_rel("sync", ("u2", 1))
        g = dma("sync", U2[:, slot * 16:(slot + 1) * 16], u_d[128 * c:128 * (c + 1), :], sd)
        P.wait("tensor", sd, g)
        P.wait_rel("tensor", ("qkb", slot))
        tv = P.op("tensor", lambda e, sl=slot: e.transpose(
            out=PS[0:16, sl * 512: sl * 512 + 128],
            in_=U2[:, sl * 16:(sl + 1) * 16], identity=IDT[:, :]), inc=tn)
        P.release(("ub", slot), tn, tv)
        P.release(("u2", 0), tn, tv)
        P.wait("scalar", tn, tv)
        ut_ev = P.op("scalar", lambda e, c_=c, sl=slot: e.copy(
            out=AR0[0:16, c_ * 128:(c_ + 1) * 128], in_=PS[0:16, sl * 512: sl * 512 + 128]), inc=se)
        P.release(("qkb", slot), se, ut_ev)
    P.wait("tensor", gd, max(fb_ld, vv_ld))
    P.wait("tensor", se, ut_ev)
    m0_ev = 0
    for n8 in range(8):
        bank = n8 % 2
        P.wait_rel("tensor", ("qkb", bank))
        v = mm(PS[0:16, bank * 512:(bank + 1) * 512], AR0[0:16, 9216:9232],
               AR0[0:16, n8 * 512:(n8 + 1) * 512], True, True, inc=tn)
        P.wait("scalar", tn, v)
        m0_ev = P.op("scalar", lambda e, n_=n8, b_=bank: e.copy(
            out=AR0[0:16, 4096 + n_ * 512: 4096 + (n_ + 1) * 512],
            in_=PS[0:16, b_ * 512:(b_ + 1) * 512]), inc=se)
        P.release(("qkb", bank), se, m0_ev)

    P.wait("tensor", se, m0_ev)
    P.wait("vector", ve, gate_ok)
    wtot_st = 0
    for c in range(NHC):
        slot = c % 2
        u1ap = U1[:, slot * 1024:(slot + 1) * 1024]
        u2ap = U2[:, slot * 1024:(slot + 1) * 1024]
        P.wait_rel("sync", ("b0w", slot))
        P.wait_rel("sync", ("u1", slot))
        P.wait_rel("sync", ("u1", slot))
        lv = dma("sync", u1ap, wfc_d[128 * c:128 * (c + 1), :], sd)
        pr = c % 2
        pb = 4 + pr * 2
        P.wait_rel("tensor", ("b0p", pr))
        if c < 2:
            P.wait_rel("tensor", ("vb", pr))
            P.wait_rel("tensor", "avbanks")
        wl_v = 0
        for n2 in range(2):
            wl_v = mm(PS[:, pb * 512 + n2 * 512: pb * 512 + (n2 + 1) * 512],
                      AR0[0:16, 4096 + c * 128: 4096 + (c + 1) * 128],
                      AR0[0:16, 8192 + n2 * 512: 8192 + (n2 + 1) * 512],
                      True, True, inc=(tn if n2 == 1 else None))
        b = slot * 8
        P.wait_rel("vector", ("b0stt", slot))
        P.wait("vector", sd, lv)
        P.op("vector", lambda e, a=u1ap, b_=b: e.reduce_sum(
            out=STT[:, b_:b_ + 1], in_=a, axis=AX.X, apply_absolute_value=True), inc=ve)
        P.wait("vector", ve, ve.n)
        P.op("vector", lambda e, b_=b: e.tensor_scalar(
            out=STT[:, b_ + 1:b_ + 2], in0=STT[:, b_:b_ + 1], scalar1=float(f32(1.0 / D)),
            scalar2=1e-5, op0=ALU.mult, op1=ALU.max), inc=ve)
        P.wait("vector", ve, ve.n)
        inv_v = P.op("vector", lambda e, b_=b: e.reciprocal(
            out=STT[:, b_ + 2:b_ + 3], in_=STT[:, b_ + 1:b_ + 2]), inc=ve)
        P.wait("scalar", ve, inv_v)
        P.wait("scalar", sd, lv)
        P.wait_rel("scalar", ("u2", slot))
        P.op("scalar", lambda e, a=u1ap, o=u2ap, b_=b: e.mul(out=o, in_=a, mul=STT[:, b_ + 2:b_ + 3]), inc=se)
        P.wait("scalar", se, se.n)
        P.op("scalar", lambda e, o=u2ap: e.add(out=o, in_=o, add=MAGIC), inc=se)
        P.wait("scalar", se, se.n)
        rm_v = P.op("scalar", lambda e, o=u2ap: e.add(out=o, in_=o, add=f32(-MAGIC)), inc=se)
        P.wait("vector", se, rm_v)
        P.op("vector", lambda e, o=u2ap: e.tensor_scalar(
            out=o, in0=o, scalar1=1.0, scalar2=-1.0, op0=ALU.min, op1=ALU.max), inc=ve)
        P.wait("vector", ve, ve.n)
        P.op("vector", lambda e, o=u2ap, b_=b: e.tensor_scalar(
            out=o, in0=o, scalar1=STT[:, b_ + 1:b_ + 2], scalar2=None, op0=ALU.mult), inc=ve)
        P.wait("vector", ve, ve.n)
        P.wait("vector", tn, wl_v)
        wt_v = P.op("vector", lambda e, a=u1ap, o=u2ap, p=pb: e.scalar_tensor_tensor(
            out=a, in0=PS[:, p * 512:(p + 2) * 512], scalar=CON[:, 128:129], in1=o,
            op0=ALU.mult, op1=ALU.add), inc=ve)
        P.release(("b0p", pr), ve, wt_v)
        P.release(("u2", slot), ve, wt_v)
        P.release(("b0stt", slot), ve, wt_v)
        P.wait("gpsimd", ve, wt_v)
        wtot_st = dma("gpsimd", wtot_bf[128 * c:128 * (c + 1), :], u1ap, gd)
        P.release(("u1", slot), gd, wtot_st)

    wp_st = 0
    prev_rm = 0
    for c in range(D // 128):
        P.wait_rel("sync", ("u1", 0))
        P.wait_rel("sync", ("u1", 1))
        l1 = dma("sync", U1[:, :], wpj_d[128 * c:128 * (c + 1), 0:2048], sd)
        P.wait_rel("sync", ("u2", 0))
        P.wait_rel("sync", ("u2", 1))
        l2 = dma("sync", U2[:, :], wpj_d[128 * c:128 * (c + 1), 2048:4096], sd)
        b = 16
        P.wait("vector", se, prev_rm)
        P.wait("vector", sd, l1)
        P.op("vector", lambda e: e.reduce_sum(out=STT[:, b:b + 1], in_=U1[:, :], axis=AX.X,
                                              apply_absolute_value=True), inc=ve)
        P.wait("vector", sd, l2)
        P.op("vector", lambda e: e.reduce_sum(out=STT[:, b + 1:b + 2], in_=U2[:, :], axis=AX.X,
                                              apply_absolute_value=True), inc=ve)
        P.wait("vector", ve, ve.n)
        P.op("vector", lambda e: e.tensor_add(out=STT[:, b + 2:b + 3], in0=STT[:, b:b + 1],
                                              in1=STT[:, b + 1:b + 2]), inc=ve)
        P.wait("vector", ve, ve.n)
        P.op("vector", lambda e: e.tensor_scalar(
            out=STT[:, b + 3:b + 4], in0=STT[:, b + 2:b + 3], scalar1=float(f32(1.0 / HID)),
            scalar2=1e-5, op0=ALU.mult, op1=ALU.max), inc=ve)
        P.wait("vector", ve, ve.n)
        inv_v = P.op("vector", lambda e: e.reciprocal(out=STT[:, b + 4:b + 5],
                                                      in_=STT[:, b + 3:b + 4]), inc=ve)
        stv = []
        for half, uap, ldv in ((0, U1[:, :], l1), (1, U2[:, :], l2)):
            P.wait("scalar", ve, inv_v)
            P.wait("scalar", sd, ldv)
            P.op("scalar", lambda e, a=uap: e.mul(out=a, in_=a, mul=STT[:, b + 4:b + 5]), inc=se)
            P.wait("scalar", se, se.n)
            P.op("scalar", lambda e, a=uap: e.add(out=a, in_=a, add=MAGIC), inc=se)
            P.wait("scalar", se, se.n)
            rm_v = P.op("scalar", lambda e, a=uap: e.add(out=a, in_=a, add=f32(-MAGIC)), inc=se)
            prev_rm = rm_v
            P.wait("vector", se, rm_v)
            P.op("vector", lambda e, a=uap: e.tensor_scalar(
                out=a, in0=a, scalar1=1.0, scalar2=-1.0, op0=ALU.min, op1=ALU.max), inc=ve)
            P.wait("vector", ve, ve.n)
            sc_v = P.op("vector", lambda e, a=uap: e.tensor_scalar(
                out=a, in0=a, scalar1=STT[:, b + 3:b + 4], scalar2=None, op0=ALU.mult), inc=ve)
            P.wait("gpsimd", ve, sc_v)
            wp_st = dma("gpsimd", wp_bf[128 * c:128 * (c + 1), half * 2048:(half + 1) * 2048], uap, gd)
            stv.append(wp_st)
        P.release_multi(("u1", 0), [(gd, stv[0])])
        P.release_multi(("u1", 1), [(gd, stv[0])])
        P.release_multi(("u2", 0), [(gd, stv[1])])
        P.release_multi(("u2", 1), [(gd, stv[1])])

    if lim < 6:
        return _finish()
    P.wait("sync", gd, wtot_st)
    P.wait("sync", tn, tn.n)
    P.wait("sync", se, se.n)
    P.wait("sync", ve, ve.n)
    P.wait("sync", av, av.n)
    wt_ld = 0
    for fi in range(NFC):
        dst = AR0 if fi < 4 else AR1
        wt_ld = dmat(dst[:, (fi % 4) * 4096:((fi % 4) + 1) * 4096],
                     wtot_bf[0:HID, 128 * fi:128 * (fi + 1)], boundary=(fi == NFC - 1))
    P.wait("sync", gd, wp_st)
    wp_ld = 0
    for c in range(NHC):
        dst = AR2 if c < 16 else AR3
        wp_ld = dmat(dst[:, (c % 16) * 1024:((c % 16) + 1) * 1024],
                     wp_bf[0:D, 128 * c:128 * (c + 1)], boundary=(c == NHC - 1))

    if lim < 7:
        return _finish()
    P.wait("tensor", sd, max(wt_ld, wp_ld))
    P.wait("tensor", sd, sd.n)
    all_gd_after_a = a_gd
    for s in range(NSEQ):
        for qb in range(8):
            xslot = qb % 2
            xoff = xslot * 2048
            P.wait_rel("sync", ("xq", xslot))
            P.wait("sync", gd, all_gd_after_a)
            xq_ld = 0
            for jf in range(NFC):
                xq_ld = dmat(AR4[:, xoff + jf * 256: xoff + (jf + 1) * 256],
                             xn2_bf[s, qb * 256:(qb + 1) * 256, 128 * jf:128 * (jf + 1)],
                             boundary=(jf == NFC - 1))
            x1lv = []
            for t2 in range(2):
                P.wait_rel("sync", ("u1", t2))
                x1lv.append(dma("sync", U1[:, t2 * 1024:(t2 + 1) * 1024],
                                x1_f[s, qb * 256 + t2 * 128: qb * 256 + (t2 + 1) * 128, :], sd))

            P.wait("tensor", sd, xq_ld)
            rr_val = {}
            fc_val = {}
            proj_state = [0]

            def emit_proj(c):
                P.wait("tensor", ve, rr_val[c])
                if c == 0:
                    P.wait_rel("tensor", "b1out")
                    if s == 0 and qb == 0:
                        P.wait_rel("tensor", ("qkb", 2))
                        P.wait_rel("tensor", ("qkb", 3))
                        P.wait_rel("tensor", ("b0p", 0))
                        P.wait_rel("tensor", ("b0p", 1))
                for t2 in range(2):
                    for n2 in range(2):
                        wsrc = AR2 if c < 16 else AR3
                        proj_state[0] = mm(
                            PS[:, (2 + t2 * 2 + n2) * 512:(2 + t2 * 2 + n2 + 1) * 512],
                            AR4[:, 8192 + c * 256 + t2 * 128: 8192 + c * 256 + (t2 + 1) * 128],
                            wsrc[:, (c % 16) * 1024 + n2 * 512:(c % 16) * 1024 + (n2 + 1) * 512],
                            c == 0, c == NHC - 1,
                            inc=(tn if (c == NHC - 1 and t2 == 1 and n2 == 1) else None))

            FCOFF = [0, 512, 3072]
            for c in range(NHC):
                hb = c % 3
                P.wait_rel("tensor", ("b1h", hb))
                if s == 0 and qb == 0 and c < 3:
                    if hb < 2:
                        P.wait_rel("tensor", ("qkb", hb))
                    else:
                        P.wait_rel("tensor", "avbanks")
                last = 0
                for fi in range(NFC):
                    wsrc = AR0 if fi < 4 else AR1
                    last = mm(PS[:, FCOFF[hb]: FCOFF[hb] + 256],
                              wsrc[:, (fi % 4) * 4096 + c * 128:(fi % 4) * 4096 + (c + 1) * 128],
                              AR4[:, xoff + fi * 256: xoff + (fi + 1) * 256],
                              fi == 0, fi == NFC - 1, inc=(tn if fi == NFC - 1 else None))
                fc_val[c] = last
                P.wait("vector", tn, last)
                if c >= 1:
                    P.wait("vector", ve, rr_val[c - 1])
                elif P.rel.get("rrstage"):
                    P.wait_rel("vector", "rrstage")
                P.op("vector", lambda e, h=hb: e.tensor_scalar_max(
                    out=RCP[:, 0:256], in0=PS[:, FCOFF[h]: FCOFF[h] + 256], scalar1=0.0), inc=ve)
                P.wait("vector", ve, ve.n)
                rr_val[c] = P.op("vector", lambda e, c_=c: e.tensor_mul(
                    out=AR4[:, 8192 + c_ * 256: 8192 + (c_ + 1) * 256],
                    in0=RCP[:, 0:256], in1=RCP[:, 0:256]), inc=ve)
                P.release(("b1h", hb), ve, rr_val[c])
                if c >= 2:
                    emit_proj(c - 2)
            emit_proj(NHC - 2)
            emit_proj(NHC - 1)
            P.release("rrstage", ve, rr_val[NHC - 1])
            proj_last = proj_state[0]
            P.release(("xq", xslot), tn, fc_val[NHC - 1])

            ap_last = 0
            for t2 in range(2):
                psap = PS[:, (2 + t2 * 2) * 512:(2 + t2 * 2 + 2) * 512]
                b = 112 + t2 * 8
                P.wait_rel("vector", ("b1stt", t2))
                P.wait("vector", tn, proj_last)
                P.op("vector", lambda e, p=psap, b_=b: e.reduce_sum(
                    out=STT[:, b_:b_ + 1], in_=p, axis=AX.X), inc=ve)
                red_v = ve.n
                P.wait("scalar", tn, proj_last)
                P.wait("scalar", ve, red_v)
                P.wait_rel("scalar", ("u2", t2))
                P.op("scalar", lambda e, p=psap, b_=b, t=t2: e.activation(
                    out=U2[:, t * 1024:(t + 1) * 1024], in_=p, func=AF.Square,
                    accum_out=STT[:, b_ + 1:b_ + 2]), inc=se)
                sq_v = se.n
                P.wait("vector", ve, red_v)
                P.wait("vector", se, sq_v)
                rdy = ln_stats(b, 1.0 / D)
                P.wait("vector", ve, rdy)
                P.wait("vector", sd, x1lv[t2])
                P.op("vector", lambda e, t=t2, b_=b: e.tensor_scalar_add(
                    out=U1[:, t * 1024:(t + 1) * 1024], in0=U1[:, t * 1024:(t + 1) * 1024],
                    scalar1=STT[:, b_ + 7:b_ + 8]), inc=ve)
                P.wait("vector", ve, ve.n)
                P.wait_rel("vector", ("u2", t2))
                y_v = P.op("vector", lambda e, p=psap, t=t2, b_=b: e.scalar_tensor_tensor(
                    out=U2[:, t * 1024:(t + 1) * 1024], in0=p, scalar=STT[:, b_ + 6:b_ + 7],
                    in1=U1[:, t * 1024:(t + 1) * 1024], op0=ALU.mult, op1=ALU.add), inc=ve)
                ap_last = y_v
                P.release(("u1", t2), ve, y_v)
                P.release(("b1stt", t2), ve, y_v)
                P.wait("gpsimd", ve, y_v)
                yst = dma("gpsimd", y_d[s, qb * 256 + t2 * 128: qb * 256 + (t2 + 1) * 128, :],
                          U2[:, t2 * 1024:(t2 + 1) * 1024], gd)
                P.release(("u2", t2), gd, yst)
            P.release("b1out", ve, ap_last)

    return _finish()


_NC_CACHE = [None]


def kernel(**inputs):
    x = np.ascontiguousarray(np.asarray(inputs["L0_attn_x"], dtype=np.float32))
    shards = [x[4 * c:4 * (c + 1)] for c in range(8)]
    base = {}
    for k in ("L0_attn_Wq", "L0_attn_Wk", "L0_attn_Wv", "L0_attn_Wo",
              "L0_ffn_fast_b", "L0_ffn_w_fc", "L0_ffn_w_proj", "L0_ffn_u", "L0_ffn_v"):
        base[k] = np.ascontiguousarray(np.asarray(inputs[k], dtype=np.float32))
    base["L0_ffn_gate"] = np.asarray(inputs["L0_ffn_gate"], dtype=np.float32).reshape(1, 1)
    if _NC_CACHE[0] is None:
        _NC_CACHE[0] = build_nc()
    nc = _NC_CACHE[0]
    in_maps = [dict(base, L0_attn_x=shards[c]) for c in range(8)]
    res = run_bass_kernel_spmd(nc, in_maps, core_ids=list(range(8)))
    return np.concatenate([res.results[c]["y"] for c in range(8)], axis=0)


# version 10
# speedup vs baseline: 1.0653x; 1.0653x over previous
import numpy as np
import ml_dtypes
from contextlib import ExitStack
import concourse.bass as bass
import concourse.mybir as mybir
from concourse.bass_utils import run_bass_kernel_spmd

F32 = mybir.dt.float32
BF16 = mybir.dt.bfloat16
AF = mybir.ActivationFunctionType
ALU = mybir.AluOpType
AX = mybir.AxisListType
f32 = np.float32
MAGIC = f32(1.5 * 2 ** 23)

NSEQ, S, D, H, DH, HID = 4, 2048, 1024, 16, 64, 4096
NT = S // 128
NQB = S // 512
NKC = S // 128
NFC = D // 128
NHC = HID // 128
NPAIR = H // 2


class Sem:
    __slots__ = ("h", "n", "name")

    def __init__(self, h, name):
        self.h, self.n, self.name = h, 0, name


class Prog:
    ENGS = ("sync", "gpsimd", "vector", "scalar", "tensor")

    def __init__(self, nc):
        self.nc = nc
        self.streams = {k: [] for k in self.ENGS}
        self.rel = {}

    def emit(self, eng, fn):
        self.streams[eng].append(fn)

    def op(self, eng, fn, inc=None, n=1):
        if inc is not None:
            inc.n += n
            v = inc.n

            def run(e, fn=fn, inc=inc, n=n):
                fn(e).then_inc(inc.h, n)
            self.emit(eng, run)
            return v
        self.emit(eng, lambda e, fn=fn: fn(e))
        return None

    def wait(self, eng, sem, val):
        if val is None or val <= 0:
            return
        self.emit(eng, lambda e, s=sem, v=val: e.wait_ge(s.h, v))

    def release(self, key, sem, val):
        self.rel[key] = [(sem, val)]

    def release_multi(self, key, pairs):
        self.rel[key] = list(pairs)

    def wait_rel(self, eng, key):
        for sem, val in self.rel.get(key, ()):
            self.wait(eng, sem, val)


def build_nc(prefix="full", nseq=NSEQ, a6_mode="full"):
    nc = bass.Bass("TRN2", target_bir_lowering=False)

    x_d = nc.dram_tensor("L0_attn_x", [NSEQ, S, D], F32, kind="ExternalInput")
    wq_d = nc.dram_tensor("L0_attn_Wq", [D, D], F32, kind="ExternalInput")
    wk_d = nc.dram_tensor("L0_attn_Wk", [D, D], F32, kind="ExternalInput")
    wv_d = nc.dram_tensor("L0_attn_Wv", [D, D], F32, kind="ExternalInput")
    wo_d = nc.dram_tensor("L0_attn_Wo", [D, D], F32, kind="ExternalInput")
    fb_d = nc.dram_tensor("L0_ffn_fast_b", [16, 16], F32, kind="ExternalInput")
    wfc_d = nc.dram_tensor("L0_ffn_w_fc", [HID, D], F32, kind="ExternalInput")
    wpj_d = nc.dram_tensor("L0_ffn_w_proj", [D, HID], F32, kind="ExternalInput")
    u_d = nc.dram_tensor("L0_ffn_u", [HID, 16], F32, kind="ExternalInput")
    v16_d = nc.dram_tensor("L0_ffn_v", [16, D], F32, kind="ExternalInput")
    gate_d = nc.dram_tensor("L0_ffn_gate", [1, 1], F32, kind="ExternalInput")
    y_d = nc.dram_tensor("y", [NSEQ, S, D], F32, kind="ExternalOutput")

    wq_bf = nc.dram_tensor("wq_bf", [D, D], BF16, kind="Internal")
    wk_bf = nc.dram_tensor("wk_bf", [D, D], BF16, kind="Internal")
    wv_bf = nc.dram_tensor("wv_bf", [D, D], BF16, kind="Internal")
    wo_bf = nc.dram_tensor("wo_bf", [D, D], BF16, kind="Internal")
    wtot_bf = nc.dram_tensor("wtot_bf", [HID, D], BF16, kind="Internal")
    wp_bf = nc.dram_tensor("wp_bf", [D, HID], BF16, kind="Internal")
    xn_bf = nc.dram_tensor("xn_bf", [NSEQ, S, D], BF16, kind="Internal")
    xn2_bf = nc.dram_tensor("xn2_bf", [NSEQ, S, D], BF16, kind="Internal")
    x1_f = nc.dram_tensor("x1_f", [NSEQ, S, D], F32, kind="Internal")

    ones_row_d = nc.inline_tensor(np.ones((1, 128), np.float32), name="ones_row_c")
    ident_d = nc.inline_tensor(np.eye(128, dtype=np.float32), name="ident_c")

    es = ExitStack()
    sb = lambda n, s, d: es.enter_context(nc.sbuf_tensor(n, s, d))

    AR0 = sb("AR0", [128, 16384], BF16)
    AR1 = sb("AR1", [128, 16384], BF16)
    AR2 = sb("AR2", [128, 16384], BF16)
    AR3 = sb("AR3", [128, 16384], BF16)
    AR4 = sb("AR4", [128, 16384], BF16)
    U1 = sb("U1", [128, 2048], F32)
    U2 = sb("U2", [128, 2048], F32)
    XST = sb("XST", [128, 2048], BF16)
    DENR = sb("DENR", [128, 512], mybir.dt.float32r)
    CONR = sb("CONR", [128, 128], mybir.dt.float32r)
    RCP = sb("RCP", [128, 1024], F32)
    STT = sb("STT", [128, 144], F32)
    DUMP = sb("DUMP", [128, 512], F32)
    CON = sb("CON", [128, 192], F32)
    ONES64 = sb("ONES64", [128, 64], BF16)
    IDT = sb("IDT", [128, 128], F32)
    PS = es.enter_context(nc.psum_tensor("PS", [128, 4096], F32))

    sems = {}
    for name in ("sd", "gd", "tn", "ve", "se", "gm", "av"):
        sems[name] = Sem(es.enter_context(nc.semaphore(name)), name)
    P = Prog(nc)
    sd, gd, tn, ve, se, gm, av = (sems[k] for k in ("sd", "gd", "tn", "ve", "se", "gm", "av"))

    AR4F = AR4[:, 12288:16384].bitcast(F32)

    def dma(eng, out, in_, sem, boundary=True):
        v = P.op(eng, lambda e, o=out, i=in_: e.dma_start(out=o, in_=i), inc=sem, n=16)
        if boundary:
            P.wait(eng, sem, v)
        return v

    def dmat(out, in_, boundary=False):
        v = P.op("sync", lambda e, o=out, i=in_: e.dma_start_transpose(out=o, in_=i), inc=sd, n=16)
        if boundary:
            P.wait("sync", sd, v)
        return v

    def mm(out, lhsT, rhs, start, stop, tile=None, inc=None):
        return P.op("tensor",
                    lambda e, o=out, l=lhsT, r=rhs, st=start, sp=stop, tp=tile:
                    e.matmul(out=o, lhsT=l, rhs=r, start=st, stop=sp, tile_position=tp),
                    inc=inc)

    def ln_stats(b, inv_n):
        c = lambda k, w=1: STT[:, b + k: b + k + w]
        P.op("vector", lambda e, c=c: e.tensor_scalar(out=c(2, 2), in0=c(0, 2), scalar1=float(f32(inv_n)),
                                                      scalar2=None, op0=ALU.mult), inc=ve)
        P.wait("vector", ve, ve.n)
        P.op("vector", lambda e, c=c: e.scalar_tensor_tensor(out=c(4), in0=c(2), scalar=c(2),
                                                             in1=c(3), op0=ALU.mult, op1=ALU.subtract), inc=ve)
        P.wait("scalar", ve, ve.n)
        P.op("scalar", lambda e, c=c: e.activation(out=c(5), in_=c(4), func=AF.Sqrt,
                                                   scale=f32(-1.0), bias=f32(1e-5)), inc=se)
        P.wait("vector", se, se.n)
        P.op("vector", lambda e, c=c: e.reciprocal(out=c(6), in_=c(5)), inc=ve)
        P.wait("vector", ve, ve.n)
        P.op("vector", lambda e, c=c: e.scalar_tensor_tensor(out=c(7), in0=c(2), scalar=-1.0,
                                                             in1=c(6), op0=ALU.mult, op1=ALU.mult), inc=ve)
        return ve.n

    def ve_rsqrt(b):
        c = lambda k, w=1: STT[:, b + k: b + k + w]
        P.op("vector", lambda e, c=c: e.tensor_scalar(
            out=c(5), in0=c(4), scalar1=-1.0, scalar2=1e-5, op0=ALU.mult, op1=ALU.add), inc=ve)
        P.wait("vector", ve, ve.n)
        vi = STT[:, b + 5: b + 6].bitcast(mybir.dt.int32)
        yi = STT[:, b + 6: b + 7].bitcast(mybir.dt.int32)
        P.op("vector", lambda e: e.tensor_scalar(out=yi, in0=vi, scalar1=1, scalar2=None,
                                                 op0=ALU.arith_shift_right), inc=ve)
        P.wait("vector", ve, ve.n)
        P.op("vector", lambda e: e.tensor_scalar(out=yi, in0=yi, scalar1=0, scalar2=None,
                                                 op0=ALU.bitwise_not), inc=ve)
        P.wait("vector", ve, ve.n)
        P.op("vector", lambda e: e.tensor_scalar(out=yi, in0=yi, scalar1=0x5F3759E0, scalar2=None,
                                                 op0=ALU.add), inc=ve)
        P.wait("vector", ve, ve.n)
        for it, (src, dst) in enumerate(((6, 8), (8, 6))):
            P.op("vector", lambda e, c=c, s_=src: e.tensor_mul(out=c(9), in0=c(s_), in1=c(s_)), inc=ve)
            P.wait("vector", ve, ve.n)
            P.op("vector", lambda e, c=c: e.tensor_mul(out=c(9), in0=c(9), in1=c(5)), inc=ve)
            P.wait("vector", ve, ve.n)
            P.op("vector", lambda e, c=c: e.tensor_scalar(
                out=c(10), in0=c(9), scalar1=-0.5, scalar2=1.5, op0=ALU.mult, op1=ALU.add), inc=ve)
            P.wait("vector", ve, ve.n)
            P.op("vector", lambda e, c=c, s_=src, d_=dst: e.tensor_mul(
                out=c(d_), in0=c(s_), in1=c(10)), inc=ve)
            P.wait("vector", ve, ve.n)
        P.op("vector", lambda e, c=c: e.scalar_tensor_tensor(out=c(7), in0=c(2), scalar=-1.0,
                                                             in1=c(6), op0=ALU.mult, op1=ALU.mult), inc=ve)
        return ve.n

    dma("sync", CON[0:1, 0:128], ones_row_d[:, :], sd)
    dma("sync", CON[64:65, 0:128], ones_row_d[:, :], sd)
    dma("sync", IDT[:, :], ident_d[:, :], sd)
    dma("sync", CON[0:1, 130:131], gate_d[:, :], sd)
    con_ld = sd.n
    P.op("gpsimd", lambda e: e.memset(ONES64[:], 1.0), inc=gm)
    ones64_ok = gm.n

    P.wait("tensor", sd, con_ld)
    gbc = mm(PS[:, 0:1], CON[0:1, 0:128], CON[0:1, 130:131], True, True, tile=(0, 0), inc=tn)
    P.wait("vector", tn, gbc)
    gate_ok = P.op("vector", lambda e: e.tensor_copy(out=CON[:, 128:129], in_=PS[:, 0:1]), inc=ve)
    P.release(("qkb", 0), ve, gate_ok)
    P.wait("vector", sd, con_ld)
    P.op("vector", lambda e: e.tensor_copy(out=CONR[0:1, :], in_=CON[0:1, 0:128]))
    conr_ok = P.op("vector", lambda e: e.tensor_copy(out=CONR[64:65, :], in_=CON[64:65, 0:128]), inc=ve)

    wcast_done = {}
    for wdram, wbf, nm in ((wq_d, wq_bf, "q"), (wk_d, wk_bf, "k"), (wv_d, wv_bf, "v"), (wo_d, wo_bf, "o")):
        for ch in range(NFC):
            slot = ch % 2
            P.wait_rel("gpsimd", ("xstw", slot))
            g = dma("gpsimd", XST[:, slot * 1024:(slot + 1) * 1024], wdram[128 * ch:128 * (ch + 1), :], gd)
            P.wait("sync", gd, g)
            sv = dma("sync", wbf[128 * ch:128 * (ch + 1), :], XST[:, slot * 1024:(slot + 1) * 1024], sd)
            P.release(("xstw", slot), sd, sv)
        wcast_done[nm] = sd.n
    xst0_rel = P.rel.pop(("xstw", 0), None)
    xst1_rel = P.rel.pop(("xstw", 1), None)
    if xst0_rel:
        P.release_multi(("xst", 0), xst0_rel)
    if xst1_rel:
        P.release_multi(("xst", 1), xst1_rel)

    last_attnN_val = 0

    STAGES = ["setup", "a1", "a23", "a4", "a6", "b0", "b1ld", "full"]
    lim = STAGES.index(prefix)

    def _finish():
        P.wait("gpsimd", gd, gd.n)
        with es, nc.Block() as blk:
            blk.sync(lambda e: [f(e) for f in P.streams["sync"]])
            blk.gpsimd(lambda e: [f(e) for f in P.streams["gpsimd"]])
            blk.vector(lambda e: [f(e) for f in P.streams["vector"]])
            blk.scalar(lambda e: [f(e) for f in P.streams["scalar"]])
            blk.tensor(lambda e: [f(e) for f in P.streams["tensor"]])
        return nc

    if lim < 1:
        return _finish()

    for s in range(nseq):
        xn_store_last = 0
        for tt in range(NT):
            slot = tt % 2
            u1ap = U1[:, slot * 1024:(slot + 1) * 1024]
            P.wait_rel("sync", ("u1", slot))
            lv = dma("sync", u1ap, x_d[s, 128 * tt:128 * (tt + 1), :], sd)
            b = (tt % 2) * 8
            P.wait_rel("vector", ("a1stt", slot))
            P.wait("vector", sd, lv)
            P.op("vector", lambda e, a=u1ap, b_=b: e.reduce_sum(out=STT[:, b_:b_ + 1], in_=a, axis=AX.X), inc=ve)
            red_v = ve.n
            P.wait("scalar", sd, lv)
            P.wait_rel("scalar", ("u2", slot))
            P.op("scalar", lambda e, a=u1ap, b_=b, sl=slot: e.activation(
                out=U2[:, sl * 1024:(sl + 1) * 1024], in_=a, func=AF.Square,
                accum_out=STT[:, b_ + 1:b_ + 2]), inc=se)
            sq_v = se.n
            P.wait("vector", ve, red_v)
            P.wait("vector", se, sq_v)
            rdy = ln_stats(b, 1.0 / D)
            P.wait_rel("scalar", ("xst", slot))
            P.wait("scalar", ve, rdy)
            ap_v = P.op("scalar", lambda e, a=u1ap, b_=b, sl=slot: e.activation(
                out=XST[:, sl * 1024:(sl + 1) * 1024], in_=a, func=AF.Identity,
                scale=STT[:, b_ + 6:b_ + 7], bias=STT[:, b_ + 7:b_ + 8]), inc=se)
            P.release_multi(("u1", slot), [(se, ap_v), (ve, red_v)])
            P.release(("a1stt", slot), se, ap_v)
            P.release(("u2", slot), se, sq_v)
            P.wait("gpsimd", se, ap_v)
            xn_store_last = dma("gpsimd", xn_bf[s, 128 * tt:128 * (tt + 1), :],
                                XST[:, slot * 1024:(slot + 1) * 1024], gd)
            P.release(("xst", slot), gd, xn_store_last)

        if lim < 2:
            continue
        P.wait("sync", tn, tn.n)
        P.wait("sync", se, se.n)
        P.wait("sync", ve, ve.n)
        P.wait("sync", av, av.n)
        P.wait("sync", gd, xn_store_last)
        xn_ld = 0
        for jf in range(NFC):
            xn_ld = dmat(AR4[:, jf * 2048:(jf + 1) * 2048], xn_bf[s, 0:S, 128 * jf:128 * (jf + 1)],
                         boundary=(jf == NFC - 1))
        P.wait("sync", sd, wcast_done["q"])
        wq_ld = 0
        for jf in range(NFC):
            wq_ld = dmat(AR3[:, jf * 1024:(jf + 1) * 1024], wq_bf[0:D, 128 * jf:128 * (jf + 1)],
                         boundary=(jf == NFC - 1))
        P.wait("sync", sd, wcast_done["k"])
        wk_ld = 0
        for jf in range(NFC):
            wk_ld = dmat(AR3[:, 8192 + jf * 1024:8192 + (jf + 1) * 1024], wk_bf[0:D, 128 * jf:128 * (jf + 1)],
                         boundary=(jf == NFC - 1))

        def qk_proj(dst_ar, w_off, w_ld_val):
            P.wait("tensor", sd, max(w_ld_val, xn_ld))
            P.wait_rel("tensor", ("slot", 0))
            P.wait_rel("tensor", ("slot", 1))
            last_ev = 0
            for oi in range(NFC):
                for qp in range(4):
                    bank = (oi * 4 + qp) % 4
                    P.wait_rel("tensor", ("qkb", bank))
                    last = 0
                    for fi in range(NFC):
                        last = mm(PS[:, bank * 512:(bank + 1) * 512],
                                  AR3[:, w_off + fi * 1024 + oi * 128: w_off + fi * 1024 + (oi + 1) * 128],
                                  AR4[:, fi * 2048 + qp * 512: fi * 2048 + (qp + 1) * 512],
                                  fi == 0, fi == NFC - 1, inc=(tn if fi == NFC - 1 else None))
                    P.wait("scalar", tn, last)
                    last_ev = P.op("scalar", lambda e, b=bank, o=oi, q=qp, d=dst_ar: e.copy(
                        out=d[:, o * 2048 + q * 512: o * 2048 + (q + 1) * 512],
                        in_=PS[:, b * 512:(b + 1) * 512]), inc=se)
                    P.release(("qkb", bank), se, last_ev)
            return last_ev

        q_ev = qk_proj(AR0, 0, wq_ld)
        q_mm_done = tn.n
        k_ev = qk_proj(AR1, 8192, wk_ld)
        k_mm_done = tn.n

        P.wait("sync", tn, q_mm_done)
        P.wait("sync", se, q_ev)
        wv_ld = 0
        for jf in range(NFC):
            wv_ld = dmat(AR3[:, jf * 1024:(jf + 1) * 1024], wv_bf[0:D, 128 * jf:128 * (jf + 1)],
                         boundary=(jf == NFC - 1))

        P.wait("tensor", sd, wv_ld)
        P.wait_rel("tensor", "bcb")
        P.wait_rel("tensor", "avbanks")
        v_ev = 0
        v_last_mm = 0
        for tt2 in range(NT):
            pair = tt2 % 2
            pb = 4 + pair * 2
            P.wait_rel("tensor", ("vb", pair))
            last = 0
            for fi in range(NFC):
                for n2 in range(2):
                    last = mm(PS[:, pb * 512 + n2 * 512: pb * 512 + (n2 + 1) * 512],
                              AR4[:, fi * 2048 + tt2 * 128: fi * 2048 + (tt2 + 1) * 128],
                              AR3[:, fi * 1024 + n2 * 512: fi * 1024 + (n2 + 1) * 512],
                              fi == 0, fi == NFC - 1,
                              inc=(tn if (fi == NFC - 1 and n2 == 1) else None))
            v_last_mm = last
            P.wait("vector", tn, last)
            v_ev = P.op("vector", lambda e, t=tt2, p=pb: e.tensor_copy(
                out=AR2[:, t * 1024:(t + 1) * 1024], in_=PS[:, p * 512:(p + 2) * 512]), inc=ve)
            P.release(("vb", pair), ve, v_ev)

        if lim < 3:
            continue
        P.wait("sync", tn, v_last_mm)
        P.wait("sync", sd, wcast_done["o"])
        wo_ld = 0
        for jf in range(NFC):
            wo_ld = dmat(AR4[:, 4096 + jf * 1024: 4096 + (jf + 1) * 1024],
                         wo_bf[0:D, 128 * jf:128 * (jf + 1)], boundary=(jf == NFC - 1))
        P.wait("tensor", se, k_ev)
        P.wait("tensor", ve, v_ev)
        P.wait("tensor", gm, ones64_ok)
        P.wait("tensor", ve, conr_ok)
        P.wait("tensor", sd, wo_ld)
        q_slot_seq = [0]
        pending = None
        AVA, AVB = 3072, 3584
        BCA = 2048
        last_sig_val = 0
        last_av_val = 0
        wo_guard = [0]
        attnN_vals = {}

        def alloc_slot():
            sq = q_slot_seq[0]
            q_slot_seq[0] += 1
            sl = sq % 2
            P.wait_rel("tensor", ("slot", sl))
            return sl

        def flush_pending():
            nonlocal pending, last_attnN_val
            if pending is None:
                return
            pj, pqb, den_val = pending
            P.wait("tensor", ve, den_val)
            P.wait_rel("tensor", "bcb")
            mm(PS[:, BCA: BCA + 512], CONR[64:65, 0:128], DENR[64:65, 0:512],
               True, True, tile=(64, 0))
            bc_val = mm(PS[:, BCA + 512: BCA + 1024], CONR[0:1, 0:128], DENR[0:1, 0:512],
                        True, True, tile=(0, 0), inc=tn)
            P.wait("vector", tn, bc_val)
            rcp_val = P.op("vector", lambda e: e.reciprocal(
                out=RCP[:, 0:1024], in_=PS[:, BCA: BCA + 1024]), inc=ve)
            P.release("bcb", ve, rcp_val)
            P.wait("vector", ve, rcp_val)
            P.wait("vector", tn, wo_guard[0])
            P.op("vector", lambda e, pj_=pj: e.tensor_mul(
                out=AR4[0:64, pj_ * 512: pj_ * 512 + 512],
                in0=PS[0:64, AVA:AVA + 512], in1=RCP[0:64, 0:512]), inc=ve)
            attnN_val = P.op("vector", lambda e, pj_=pj: e.tensor_mul(
                out=AR4[64:128, pj_ * 512: pj_ * 512 + 512],
                in0=PS[64:128, AVB:AVB + 512], in1=RCP[64:128, 512:1024]), inc=ve)
            P.release("avbanks", ve, attnN_val)
            attnN_vals[pj] = attnN_val
            last_attnN_val = attnN_val
            pending = None

        pending_av = None
        qb_av_guard = av.n

        def emit_av_chunk(pj, pqb, psig, c):
            nonlocal last_av_val
            av_last_loc = 0
            for kc in range(4 * c, 4 * c + 4):
                P.wait("tensor", se, psig[kc])
                if kc == 0:
                    P.wait_rel("tensor", "avbanks")
                wA = AR3[:, kc * 1024: kc * 1024 + 512]
                wB = AR3[:, kc * 1024 + 512: (kc + 1) * 1024]
                mm(PS[0:64, AVA:AVA + 512],
                   AR2[:, kc * 1024 + (2 * pj) * 64: kc * 1024 + (2 * pj) * 64 + 64],
                   wA, kc == 0, kc == NKC - 1, tile=(0, 0), inc=av)
                mm(PS[64:128, AVA:AVA + 512], ONES64[:, :], wA, kc == 0, kc == NKC - 1, tile=(0, 64), inc=av)
                mm(PS[0:64, AVB:AVB + 512], ONES64[:, :], wB, kc == 0, kc == NKC - 1, tile=(0, 0), inc=av)
                av_last_loc = mm(PS[64:128, AVB:AVB + 512],
                                 AR2[:, kc * 1024 + (2 * pj + 1) * 64: kc * 1024 + (2 * pj + 1) * 64 + 64],
                                 wB, kc == 0, kc == NKC - 1, tile=(0, 64), inc=av)
            last_av_val = av_last_loc
            return av_last_loc

        def emit_den_and_pend(pj, pqb, av_last_loc):
            nonlocal pending
            P.wait("vector", av, av_last_loc)
            P.op("vector", lambda e: e.tensor_scalar_add(
                out=DENR[64:65, 0:512], in0=PS[64:65, AVA:AVA + 512], scalar1=1e-6))
            den_val = P.op("vector", lambda e: e.tensor_scalar_add(
                out=DENR[0:1, 0:512], in0=PS[0:1, AVB:AVB + 512], scalar1=1e-6), inc=ve)
            pending = (pj, pqb, den_val)

        def emit_score_kc(j, qb, kc, sig_val, av_guard):
            nonlocal last_sig_val
            sl = alloc_slot()
            mm(PS[:, sl * 1024: sl * 1024 + 512],
               AR1[0:64, j * 2048 + kc * 128: j * 2048 + (kc + 1) * 128],
               AR0[0:64, j * 2048 + qb * 512: j * 2048 + (qb + 1) * 512],
               True, True, tile=(0, 0))
            sc_val = mm(PS[:, sl * 1024 + 512: sl * 1024 + 1024],
                        AR1[64:128, j * 2048 + kc * 128: j * 2048 + (kc + 1) * 128],
                        AR0[64:128, j * 2048 + qb * 512: j * 2048 + (qb + 1) * 512],
                        True, True, tile=(64, 0), inc=tn)
            P.wait("scalar", tn, sc_val)
            if av_guard > 0:
                P.wait("scalar", av, av_guard)
            sv = P.op("scalar", lambda e, sl_=sl, kc_=kc: e.activation(
                out=AR3[:, kc_ * 1024:(kc_ + 1) * 1024],
                in_=PS[:, sl_ * 1024:(sl_ + 1) * 1024],
                func=AF.Sigmoid, scale=f32(0.125)), inc=se)
            sig_val[kc] = sv
            last_sig_val = sv
            P.release(("slot", sl), se, sv)

        SC_BEFORE = {0: 1, 1: 3, 2: 5, 3: 7}
        WO_AFTER = {3: 0, 6: 1, 9: 2, 12: 3}
        pending_wo = None
        epi_state = {}
        for qb in range(NQB):
            for j in range(NPAIR):
                flush_pending()
                if pending_wo is not None and 1 <= j <= 6:
                    st1f, st2f, st3f = pending_wo[1]
                    t3_ = j - 3
                    if 0 <= t3_ <= 3:
                        st3f(t3_, epi_state.pop(t3_ + 100))
                    t2_ = j - 2
                    if 0 <= t2_ <= 3:
                        epi_state[t2_ + 100] = st2f(t2_, epi_state.pop(t2_))
                    t1_ = j - 1
                    if 0 <= t1_ <= 3:
                        epi_state[t1_] = st1f(t1_)
                    if j == 6:
                        pending_wo = None
                sig_val = {}
                nsc = 0
                av_base = av.n if pending_av is not None else -1
                for c in range(4):
                    while nsc < SC_BEFORE[c]:
                        g = (av_base + 4 * (nsc + 1)) if av_base >= 0 else qb_av_guard
                        emit_score_kc(j, qb, nsc, sig_val, g)
                        nsc += 1
                        if pending_wo is not None and j == 0 and nsc in WO_AFTER:
                            pending_wo[0](WO_AFTER[nsc])
                    if pending_av is not None:
                        pj, psig = pending_av
                        avl = emit_av_chunk(pj, qb, psig, c)
                        if c == 3:
                            emit_den_and_pend(pj, qb, avl)
                            flush_pending()
                while nsc < NKC:
                    g = (av_base + 4 * (nsc + 1)) if av_base >= 0 else qb_av_guard
                    emit_score_kc(j, qb, nsc, sig_val, g)
                    nsc += 1
                    if pending_wo is not None and j == 0 and nsc in WO_AFTER:
                        pending_wo[0](WO_AFTER[nsc])
                pending_av = (j, dict(sig_val))
            pj, psig = pending_av
            for c in range(4):
                avl = emit_av_chunk(pj, qb, psig, c)
            emit_den_and_pend(pj, qb, avl)
            flush_pending()
            pending_av = None
            qb_av_guard = av.n
            anv = dict(attnN_vals)

            def make_wo(qb=qb, anv=anv):
                def ao_ap(tl):
                    if tl < 2:
                        return U2[:, tl * 1024:(tl + 1) * 1024]
                    return AR4F[:, (tl - 2) * 1024:(tl - 1) * 1024]

                def ao_key(tl):
                    return ("u2", tl) if tl < 2 else ("aof", tl - 2)

                def emit_wo(tl):
                    if tl == 0:
                        P.wait_rel("tensor", "bcb")
                    lasts = [0, 0]
                    for h in range(2):
                        bk = 2048 + ((2 * tl + h) % 2) * 512
                        P.wait_rel("tensor", ("wob", (2 * tl + h) % 2))
                        for jf in range(NFC):
                            P.wait("tensor", ve, anv.get(jf, 0)) if tl == 0 and h == 0 else None
                            lasts[h] = mm(PS[:, bk: bk + 512],
                                          AR4[:, jf * 512 + tl * 128: jf * 512 + (tl + 1) * 128],
                                          AR4[:, 4096 + jf * 1024 + h * 512: 4096 + jf * 1024 + (h + 1) * 512],
                                          jf == 0, jf == NFC - 1,
                                          inc=(tn if jf == NFC - 1 else None))
                    b = 16 + tl * 12
                    ao = ao_ap(tl)
                    P.wait_rel("vector", ao_key(tl))
                    P.wait_rel("vector", ("a4stt1", tl))
                    cps = []
                    for h in range(2):
                        bk = 2048 + ((2 * tl + h) % 2) * 512
                        P.wait("vector", tn, lasts[h])
                        cps.append(P.op("vector", lambda e, h_=h, a=ao, b_=b, bk_=bk: e.tensor_scalar(
                            out=a[:, h_ * 512:(h_ + 1) * 512],
                            in0=PS[:, bk_: bk_ + 512],
                            scalar1=1.0, scalar2=0.0, op0=ALU.mult, op1=ALU.add,
                            accum_out=STT[:, b_ + 8 + 2 * h_: b_ + 9 + 2 * h_]), inc=ve))
                        P.release(("wob", (2 * tl + h) % 2), ve, cps[h])
                    if tl == 3:
                        P.release("bcb", ve, cps[1])
                    return ao
                return emit_wo

            def make_epi(s=s, qb=qb):
                def ao_ap(tl):
                    if tl < 2:
                        return U2[:, tl * 1024:(tl + 1) * 1024]
                    return AR4F[:, (tl - 2) * 1024:(tl - 1) * 1024]

                def ao_key(tl):
                    return ("u2", tl) if tl < 2 else ("aof", tl - 2)

                def st1(tl):
                    tt = qb * 4 + tl
                    uslot = tl % 2
                    ao = ao_ap(tl)
                    b = 16 + tl * 12
                    c = lambda k, w=1: STT[:, b + k: b + k + w]
                    P.wait_rel("sync", ("u1", uslot))
                    xlv = dma("sync", U1[:, uslot * 1024:(uslot + 1) * 1024],
                              x_d[s, 128 * tt:128 * (tt + 1), :], sd)
                    for h in range(2):
                        P.op("vector", lambda e, h_=h, a=ao: e.tensor_mul(
                            out=DUMP[:, :], in0=a[:, h_ * 512:(h_ + 1) * 512],
                            in1=a[:, h_ * 512:(h_ + 1) * 512]), inc=ve)
                        P.wait("vector", ve, ve.n)
                        P.op("vector", lambda e, h_=h, b_=b: e.reduce_sum(
                            out=STT[:, b_ + 9 + 2 * h_: b_ + 10 + 2 * h_], in_=DUMP[:, :], axis=AX.X), inc=ve)
                        P.wait("vector", ve, ve.n)
                    P.op("vector", lambda e, b_=b: e.tensor_add(
                        out=STT[:, b_: b_ + 2], in0=STT[:, b_ + 8: b_ + 10],
                        in1=STT[:, b_ + 10: b_ + 12]), inc=ve)
                    P.wait("vector", ve, ve.n)
                    P.op("vector", lambda e, c=c: e.tensor_scalar(out=c(2, 2), in0=c(0, 2),
                                                                  scalar1=float(f32(1.0 / D)),
                                                                  scalar2=None, op0=ALU.mult), inc=ve)
                    P.wait("vector", ve, ve.n)
                    vn = P.op("vector", lambda e, c=c: e.scalar_tensor_tensor(
                        out=c(4), in0=c(2), scalar=c(2), in1=c(3),
                        op0=ALU.mult, op1=ALU.subtract), inc=ve)
                    return (xlv, vn)

                def st2(tl, state):
                    xlv, vn = state
                    tt = qb * 4 + tl
                    uslot = tl % 2
                    ao = ao_ap(tl)
                    u1ap = U1[:, uslot * 1024:(uslot + 1) * 1024]
                    b = 16 + tl * 12
                    b2 = 64 + tl * 12
                    P.wait("vector", ve, vn)
                    ve_rsqrt(b)
                    P.wait("vector", ve, ve.n)
                    P.wait("vector", sd, xlv)
                    P.op("vector", lambda e, a=u1ap, b_=b: e.tensor_scalar_add(
                        out=a, in0=a, scalar1=STT[:, b_ + 7:b_ + 8]), inc=ve)
                    P.wait("vector", ve, ve.n)
                    x1_v = P.op("vector", lambda e, a=ao, xp=u1ap, b_=b: e.scalar_tensor_tensor(
                        out=a, in0=a, scalar=STT[:, b_ + 6:b_ + 7], in1=xp,
                        op0=ALU.mult, op1=ALU.add), inc=ve)
                    P.release(("a4stt1", tl), ve, x1_v)
                    P.release(("u1", uslot), ve, x1_v)
                    P.wait("gpsimd", ve, x1_v)
                    x1st = dma("gpsimd", x1_f[s, 128 * tt:128 * (tt + 1), :], ao, gd)
                    P.wait_rel("vector", ("a4stt2", tl))
                    P.wait("vector", ve, x1_v)
                    P.op("vector", lambda e, a=ao, b_=b2: e.reduce_sum(
                        out=STT[:, b_:b_ + 1], in_=a, axis=AX.X), inc=ve)
                    P.wait("vector", ve, ve.n)
                    for h in range(2):
                        P.op("vector", lambda e, h_=h, a=ao: e.tensor_mul(
                            out=DUMP[:, :], in0=a[:, h_ * 512:(h_ + 1) * 512],
                            in1=a[:, h_ * 512:(h_ + 1) * 512]), inc=ve)
                        P.wait("vector", ve, ve.n)
                        P.op("vector", lambda e, h_=h, b_=b2: e.reduce_sum(
                            out=STT[:, b_ + 9 + 2 * h_: b_ + 10 + 2 * h_], in_=DUMP[:, :], axis=AX.X), inc=ve)
                        P.wait("vector", ve, ve.n)
                    P.op("vector", lambda e, b_=b2: e.tensor_add(
                        out=STT[:, b_ + 1: b_ + 2], in0=STT[:, b_ + 9: b_ + 10],
                        in1=STT[:, b_ + 11: b_ + 12]), inc=ve)
                    P.wait("vector", ve, ve.n)
                    c2 = lambda k, w=1: STT[:, b2 + k: b2 + k + w]
                    P.op("vector", lambda e, c=c2: e.tensor_scalar(out=c(2, 2), in0=c(0, 2),
                                                                   scalar1=float(f32(1.0 / D)),
                                                                   scalar2=None, op0=ALU.mult), inc=ve)
                    P.wait("vector", ve, ve.n)
                    vn2 = P.op("vector", lambda e, c=c2: e.scalar_tensor_tensor(
                        out=c(4), in0=c(2), scalar=c(2), in1=c(3),
                        op0=ALU.mult, op1=ALU.subtract), inc=ve)
                    return (x1st, vn2)

                def st3(tl, state):
                    x1st, vn2 = state
                    tt = qb * 4 + tl
                    uslot = tl % 2
                    ao = ao_ap(tl)
                    b2 = 64 + tl * 12
                    P.wait("vector", ve, vn2)
                    ve_rsqrt(b2)
                    P.wait("vector", ve, ve.n)
                    P.wait_rel("vector", ("xst", uslot))
                    ap2_v = P.op("vector", lambda e, a=ao, b_=b2, sl=uslot: e.tensor_scalar(
                        out=XST[:, sl * 1024:(sl + 1) * 1024], in0=a,
                        scalar1=STT[:, b_ + 2:b_ + 3] if False else STT[:, b2 + 2:b2 + 3],
                        scalar2=STT[:, b2 + 6:b2 + 7],
                        op0=ALU.subtract, op1=ALU.mult), inc=ve)
                    P.release(("a4stt2", tl), ve, ap2_v)
                    P.release_multi(ao_key(tl), [(gd, x1st), (ve, ap2_v)])
                    P.wait("gpsimd", ve, ap2_v)
                    g = dma("gpsimd", xn2_bf[s, 128 * tt:128 * (tt + 1), :],
                            XST[:, uslot * 1024:(uslot + 1) * 1024], gd)
                    P.release(("xst", uslot), gd, g)
                return st1, st2, st3

            pending_wo = (make_wo(), make_epi())
            epi_state = {}
            attnN_vals = {}
        if pending_wo is not None:
            for tl in range(4):
                pending_wo[0](tl)
            st1f, st2f, st3f = pending_wo[1]
            for step in range(6):
                t3_ = step - 2
                if 0 <= t3_ <= 3:
                    st3f(t3_, epi_state.pop(t3_ + 100))
                t2_ = step - 1
                if 0 <= t2_ <= 3:
                    epi_state[t2_ + 100] = st2f(t2_, epi_state.pop(t2_))
                if 0 <= step <= 3:
                    epi_state[step] = st1f(step)
            pending_wo = None
            epi_state = {}

    if lim < 5:
        return _finish()
    a_tn, a_se, a_ve, a_gd, a_sd = tn.n, se.n, ve.n, gd.n, sd.n
    a_av = av.n
    for eng in ("gpsimd", "sync", "tensor"):
        P.wait(eng, tn, a_tn)
        P.wait(eng, se, a_se)
        P.wait(eng, ve, a_ve)
        P.wait(eng, av, a_av)
    fb_ld = dma("gpsimd", AR0[0:16, 9216:9232], fb_d[:, :], gd)
    vv_ld = dma("gpsimd", AR0[0:16, 8192:9216], v16_d[:, :], gd)
    ut_ev = 0
    for c in range(NHC):
        slot = c % 2
        P.wait_rel("sync", ("ub", slot))
        if c < 2:
            P.wait_rel("sync", ("u2", 0))
            P.wait_rel("sync", ("u2", 1))
        g = dma("sync", U2[:, slot * 16:(slot + 1) * 16], u_d[128 * c:128 * (c + 1), :], sd)
        P.wait("tensor", sd, g)
        P.wait_rel("tensor", ("qkb", slot))
        tv = P.op("tensor", lambda e, sl=slot: e.transpose(
            out=PS[0:16, sl * 512: sl * 512 + 128],
            in_=U2[:, sl * 16:(sl + 1) * 16], identity=IDT[:, :]), inc=tn)
        P.release(("ub", slot), tn, tv)
        P.release(("u2", 0), tn, tv)
        P.wait("scalar", tn, tv)
        ut_ev = P.op("scalar", lambda e, c_=c, sl=slot: e.copy(
            out=AR0[0:16, c_ * 128:(c_ + 1) * 128], in_=PS[0:16, sl * 512: sl * 512 + 128]), inc=se)
        P.release(("qkb", slot), se, ut_ev)
    P.wait("tensor", gd, max(fb_ld, vv_ld))
    P.wait("tensor", se, ut_ev)
    m0_ev = 0
    for n8 in range(8):
        bank = n8 % 2
        P.wait_rel("tensor", ("qkb", bank))
        v = mm(PS[0:16, bank * 512:(bank + 1) * 512], AR0[0:16, 9216:9232],
               AR0[0:16, n8 * 512:(n8 + 1) * 512], True, True, inc=tn)
        P.wait("scalar", tn, v)
        m0_ev = P.op("scalar", lambda e, n_=n8, b_=bank: e.copy(
            out=AR0[0:16, 4096 + n_ * 512: 4096 + (n_ + 1) * 512],
            in_=PS[0:16, b_ * 512:(b_ + 1) * 512]), inc=se)
        P.release(("qkb", bank), se, m0_ev)

    P.wait("tensor", se, m0_ev)
    P.wait("vector", ve, gate_ok)
    wtot_st = 0
    for c in range(NHC):
        slot = c % 2
        u1ap = U1[:, slot * 1024:(slot + 1) * 1024]
        u2ap = U2[:, slot * 1024:(slot + 1) * 1024]
        P.wait_rel("sync", ("b0w", slot))
        P.wait_rel("sync", ("u1", slot))
        P.wait_rel("sync", ("u1", slot))
        lv = dma("sync", u1ap, wfc_d[128 * c:128 * (c + 1), :], sd)
        pr = c % 2
        pb = 4 + pr * 2
        P.wait_rel("tensor", ("b0p", pr))
        if c < 2:
            P.wait_rel("tensor", ("vb", pr))
            P.wait_rel("tensor", "avbanks")
        wl_v = 0
        for n2 in range(2):
            wl_v = mm(PS[:, pb * 512 + n2 * 512: pb * 512 + (n2 + 1) * 512],
                      AR0[0:16, 4096 + c * 128: 4096 + (c + 1) * 128],
                      AR0[0:16, 8192 + n2 * 512: 8192 + (n2 + 1) * 512],
                      True, True, inc=(tn if n2 == 1 else None))
        b = slot * 8
        P.wait_rel("vector", ("b0stt", slot))
        P.wait("vector", sd, lv)
        P.op("vector", lambda e, a=u1ap, b_=b: e.reduce_sum(
            out=STT[:, b_:b_ + 1], in_=a, axis=AX.X, apply_absolute_value=True), inc=ve)
        P.wait("vector", ve, ve.n)
        P.op("vector", lambda e, b_=b: e.tensor_scalar(
            out=STT[:, b_ + 1:b_ + 2], in0=STT[:, b_:b_ + 1], scalar1=float(f32(1.0 / D)),
            scalar2=1e-5, op0=ALU.mult, op1=ALU.max), inc=ve)
        P.wait("vector", ve, ve.n)
        inv_v = P.op("vector", lambda e, b_=b: e.reciprocal(
            out=STT[:, b_ + 2:b_ + 3], in_=STT[:, b_ + 1:b_ + 2]), inc=ve)
        P.wait("scalar", ve, inv_v)
        P.wait("scalar", sd, lv)
        P.wait_rel("scalar", ("u2", slot))
        P.op("scalar", lambda e, a=u1ap, o=u2ap, b_=b: e.mul(out=o, in_=a, mul=STT[:, b_ + 2:b_ + 3]), inc=se)
        P.wait("scalar", se, se.n)
        P.op("scalar", lambda e, o=u2ap: e.add(out=o, in_=o, add=MAGIC), inc=se)
        P.wait("scalar", se, se.n)
        rm_v = P.op("scalar", lambda e, o=u2ap: e.add(out=o, in_=o, add=f32(-MAGIC)), inc=se)
        P.wait("vector", se, rm_v)
        P.op("vector", lambda e, o=u2ap: e.tensor_scalar(
            out=o, in0=o, scalar1=1.0, scalar2=-1.0, op0=ALU.min, op1=ALU.max), inc=ve)
        P.wait("vector", ve, ve.n)
        P.op("vector", lambda e, o=u2ap, b_=b: e.tensor_scalar(
            out=o, in0=o, scalar1=STT[:, b_ + 1:b_ + 2], scalar2=None, op0=ALU.mult), inc=ve)
        P.wait("vector", ve, ve.n)
        P.wait("vector", tn, wl_v)
        wt_v = P.op("vector", lambda e, a=u1ap, o=u2ap, p=pb: e.scalar_tensor_tensor(
            out=a, in0=PS[:, p * 512:(p + 2) * 512], scalar=CON[:, 128:129], in1=o,
            op0=ALU.mult, op1=ALU.add), inc=ve)
        P.release(("b0p", pr), ve, wt_v)
        P.release(("u2", slot), ve, wt_v)
        P.release(("b0stt", slot), ve, wt_v)
        P.wait("gpsimd", ve, wt_v)
        wtot_st = dma("gpsimd", wtot_bf[128 * c:128 * (c + 1), :], u1ap, gd)
        P.release(("u1", slot), gd, wtot_st)

    wp_st = 0
    prev_rm = 0
    for c in range(D // 128):
        P.wait_rel("sync", ("u1", 0))
        P.wait_rel("sync", ("u1", 1))
        l1 = dma("sync", U1[:, :], wpj_d[128 * c:128 * (c + 1), 0:2048], sd)
        P.wait_rel("sync", ("u2", 0))
        P.wait_rel("sync", ("u2", 1))
        l2 = dma("sync", U2[:, :], wpj_d[128 * c:128 * (c + 1), 2048:4096], sd)
        b = 16
        P.wait("vector", se, prev_rm)
        P.wait("vector", sd, l1)
        P.op("vector", lambda e: e.reduce_sum(out=STT[:, b:b + 1], in_=U1[:, :], axis=AX.X,
                                              apply_absolute_value=True), inc=ve)
        P.wait("vector", sd, l2)
        P.op("vector", lambda e: e.reduce_sum(out=STT[:, b + 1:b + 2], in_=U2[:, :], axis=AX.X,
                                              apply_absolute_value=True), inc=ve)
        P.wait("vector", ve, ve.n)
        P.op("vector", lambda e: e.tensor_add(out=STT[:, b + 2:b + 3], in0=STT[:, b:b + 1],
                                              in1=STT[:, b + 1:b + 2]), inc=ve)
        P.wait("vector", ve, ve.n)
        P.op("vector", lambda e: e.tensor_scalar(
            out=STT[:, b + 3:b + 4], in0=STT[:, b + 2:b + 3], scalar1=float(f32(1.0 / HID)),
            scalar2=1e-5, op0=ALU.mult, op1=ALU.max), inc=ve)
        P.wait("vector", ve, ve.n)
        inv_v = P.op("vector", lambda e: e.reciprocal(out=STT[:, b + 4:b + 5],
                                                      in_=STT[:, b + 3:b + 4]), inc=ve)
        stv = []
        for half, uap, ldv in ((0, U1[:, :], l1), (1, U2[:, :], l2)):
            P.wait("scalar", ve, inv_v)
            P.wait("scalar", sd, ldv)
            P.op("scalar", lambda e, a=uap: e.mul(out=a, in_=a, mul=STT[:, b + 4:b + 5]), inc=se)
            P.wait("scalar", se, se.n)
            P.op("scalar", lambda e, a=uap: e.add(out=a, in_=a, add=MAGIC), inc=se)
            P.wait("scalar", se, se.n)
            rm_v = P.op("scalar", lambda e, a=uap: e.add(out=a, in_=a, add=f32(-MAGIC)), inc=se)
            prev_rm = rm_v
            P.wait("vector", se, rm_v)
            P.op("vector", lambda e, a=uap: e.tensor_scalar(
                out=a, in0=a, scalar1=1.0, scalar2=-1.0, op0=ALU.min, op1=ALU.max), inc=ve)
            P.wait("vector", ve, ve.n)
            sc_v = P.op("vector", lambda e, a=uap: e.tensor_scalar(
                out=a, in0=a, scalar1=STT[:, b + 3:b + 4], scalar2=None, op0=ALU.mult), inc=ve)
            P.wait("gpsimd", ve, sc_v)
            wp_st = dma("gpsimd", wp_bf[128 * c:128 * (c + 1), half * 2048:(half + 1) * 2048], uap, gd)
            stv.append(wp_st)
        P.release_multi(("u1", 0), [(gd, stv[0])])
        P.release_multi(("u1", 1), [(gd, stv[0])])
        P.release_multi(("u2", 0), [(gd, stv[1])])
        P.release_multi(("u2", 1), [(gd, stv[1])])

    if lim < 6:
        return _finish()
    P.wait("sync", gd, wtot_st)
    P.wait("sync", tn, tn.n)
    P.wait("sync", se, se.n)
    P.wait("sync", ve, ve.n)
    P.wait("sync", av, av.n)
    wt_ld = 0
    for fi in range(NFC):
        dst = AR0 if fi < 4 else AR1
        wt_ld = dmat(dst[:, (fi % 4) * 4096:((fi % 4) + 1) * 4096],
                     wtot_bf[0:HID, 128 * fi:128 * (fi + 1)], boundary=(fi == NFC - 1))
    P.wait("sync", gd, wp_st)
    wp_ld = 0
    for c in range(NHC):
        dst = AR2 if c < 16 else AR3
        wp_ld = dmat(dst[:, (c % 16) * 1024:((c % 16) + 1) * 1024],
                     wp_bf[0:D, 128 * c:128 * (c + 1)], boundary=(c == NHC - 1))

    if lim < 7:
        return _finish()
    P.wait("tensor", sd, max(wt_ld, wp_ld))
    P.wait("tensor", sd, sd.n)
    all_gd_after_a = a_gd
    for s in range(NSEQ):
        for qb in range(8):
            xslot = qb % 2
            xoff = xslot * 2048
            P.wait_rel("sync", ("xq", xslot))
            P.wait("sync", gd, all_gd_after_a)
            xq_ld = 0
            for jf in range(NFC):
                xq_ld = dmat(AR4[:, xoff + jf * 256: xoff + (jf + 1) * 256],
                             xn2_bf[s, qb * 256:(qb + 1) * 256, 128 * jf:128 * (jf + 1)],
                             boundary=(jf == NFC - 1))
            x1lv = []
            for t2 in range(2):
                P.wait_rel("sync", ("u1", t2))
                x1lv.append(dma("sync", U1[:, t2 * 1024:(t2 + 1) * 1024],
                                x1_f[s, qb * 256 + t2 * 128: qb * 256 + (t2 + 1) * 128, :], sd))

            P.wait("tensor", sd, xq_ld)
            rr_val = {}
            fc_val = {}
            proj_state = [0]

            def emit_proj(c):
                P.wait("tensor", ve, rr_val[c])
                if c == 0:
                    P.wait_rel("tensor", "b1out")
                    if s == 0 and qb == 0:
                        P.wait_rel("tensor", ("qkb", 2))
                        P.wait_rel("tensor", ("qkb", 3))
                        P.wait_rel("tensor", ("b0p", 0))
                        P.wait_rel("tensor", ("b0p", 1))
                for t2 in range(2):
                    for n2 in range(2):
                        wsrc = AR2 if c < 16 else AR3
                        proj_state[0] = mm(
                            PS[:, (2 + t2 * 2 + n2) * 512:(2 + t2 * 2 + n2 + 1) * 512],
                            AR4[:, 8192 + c * 256 + t2 * 128: 8192 + c * 256 + (t2 + 1) * 128],
                            wsrc[:, (c % 16) * 1024 + n2 * 512:(c % 16) * 1024 + (n2 + 1) * 512],
                            c == 0, c == NHC - 1,
                            inc=(tn if (c == NHC - 1 and t2 == 1 and n2 == 1) else None))

            FCOFF = [0, 512, 3072]
            for c in range(NHC):
                hb = c % 3
                P.wait_rel("tensor", ("b1h", hb))
                if s == 0 and qb == 0 and c < 3:
                    if hb < 2:
                        P.wait_rel("tensor", ("qkb", hb))
                    else:
                        P.wait_rel("tensor", "avbanks")
                last = 0
                for fi in range(NFC):
                    wsrc = AR0 if fi < 4 else AR1
                    last = mm(PS[:, FCOFF[hb]: FCOFF[hb] + 256],
                              wsrc[:, (fi % 4) * 4096 + c * 128:(fi % 4) * 4096 + (c + 1) * 128],
                              AR4[:, xoff + fi * 256: xoff + (fi + 1) * 256],
                              fi == 0, fi == NFC - 1, inc=(tn if fi == NFC - 1 else None))
                fc_val[c] = last
                P.wait("vector", tn, last)
                if c >= 1:
                    P.wait("vector", ve, rr_val[c - 1])
                elif P.rel.get("rrstage"):
                    P.wait_rel("vector", "rrstage")
                P.op("vector", lambda e, h=hb: e.tensor_scalar_max(
                    out=RCP[:, 0:256], in0=PS[:, FCOFF[h]: FCOFF[h] + 256], scalar1=0.0), inc=ve)
                P.wait("vector", ve, ve.n)
                rr_val[c] = P.op("vector", lambda e, c_=c: e.tensor_mul(
                    out=AR4[:, 8192 + c_ * 256: 8192 + (c_ + 1) * 256],
                    in0=RCP[:, 0:256], in1=RCP[:, 0:256]), inc=ve)
                P.release(("b1h", hb), ve, rr_val[c])
                if c >= 2:
                    emit_proj(c - 2)
            emit_proj(NHC - 2)
            emit_proj(NHC - 1)
            P.release("rrstage", ve, rr_val[NHC - 1])
            proj_last = proj_state[0]
            P.release(("xq", xslot), tn, fc_val[NHC - 1])

            ap_last = 0
            for t2 in range(2):
                psap = PS[:, (2 + t2 * 2) * 512:(2 + t2 * 2 + 2) * 512]
                b = 112 + t2 * 8
                P.wait_rel("vector", ("b1stt", t2))
                P.wait("vector", tn, proj_last)
                P.op("vector", lambda e, p=psap, b_=b: e.reduce_sum(
                    out=STT[:, b_:b_ + 1], in_=p, axis=AX.X), inc=ve)
                red_v = ve.n
                P.wait("scalar", tn, proj_last)
                P.wait("scalar", ve, red_v)
                P.wait_rel("scalar", ("u2", t2))
                P.op("scalar", lambda e, p=psap, b_=b, t=t2: e.activation(
                    out=U2[:, t * 1024:(t + 1) * 1024], in_=p, func=AF.Square,
                    accum_out=STT[:, b_ + 1:b_ + 2]), inc=se)
                sq_v = se.n
                P.wait("vector", ve, red_v)
                P.wait("vector", se, sq_v)
                rdy = ln_stats(b, 1.0 / D)
                P.wait("vector", ve, rdy)
                P.wait("vector", sd, x1lv[t2])
                P.op("vector", lambda e, t=t2, b_=b: e.tensor_scalar_add(
                    out=U1[:, t * 1024:(t + 1) * 1024], in0=U1[:, t * 1024:(t + 1) * 1024],
                    scalar1=STT[:, b_ + 7:b_ + 8]), inc=ve)
                P.wait("vector", ve, ve.n)
                P.wait_rel("vector", ("u2", t2))
                y_v = P.op("vector", lambda e, p=psap, t=t2, b_=b: e.scalar_tensor_tensor(
                    out=U2[:, t * 1024:(t + 1) * 1024], in0=p, scalar=STT[:, b_ + 6:b_ + 7],
                    in1=U1[:, t * 1024:(t + 1) * 1024], op0=ALU.mult, op1=ALU.add), inc=ve)
                ap_last = y_v
                P.release(("u1", t2), ve, y_v)
                P.release(("b1stt", t2), ve, y_v)
                P.wait("gpsimd", ve, y_v)
                yst = dma("gpsimd", y_d[s, qb * 256 + t2 * 128: qb * 256 + (t2 + 1) * 128, :],
                          U2[:, t2 * 1024:(t2 + 1) * 1024], gd)
                P.release(("u2", t2), gd, yst)
            P.release("b1out", ve, ap_last)

    return _finish()


_NC_CACHE = [None]


def kernel(**inputs):
    x = np.ascontiguousarray(np.asarray(inputs["L0_attn_x"], dtype=np.float32))
    shards = [x[4 * c:4 * (c + 1)] for c in range(8)]
    base = {}
    for k in ("L0_attn_Wq", "L0_attn_Wk", "L0_attn_Wv", "L0_attn_Wo",
              "L0_ffn_fast_b", "L0_ffn_w_fc", "L0_ffn_w_proj", "L0_ffn_u", "L0_ffn_v"):
        base[k] = np.ascontiguousarray(np.asarray(inputs[k], dtype=np.float32))
    base["L0_ffn_gate"] = np.asarray(inputs["L0_ffn_gate"], dtype=np.float32).reshape(1, 1)
    if _NC_CACHE[0] is None:
        _NC_CACHE[0] = build_nc()
    nc = _NC_CACHE[0]
    in_maps = [dict(base, L0_attn_x=shards[c]) for c in range(8)]
    res = run_bass_kernel_spmd(nc, in_maps, core_ids=list(range(8)))
    return np.concatenate([res.results[c]["y"] for c in range(8)], axis=0)
